# version 02
# baseline (speedup 1.0000x reference)
import math
import numpy as np
import ml_dtypes
import concourse.bass as bass
import concourse.mybir as mybir
from concourse.bass_utils import run_bass_kernel_spmd

F32 = mybir.dt.float32
BF16 = mybir.dt.bfloat16
AF = mybir.ActivationFunctionType
ALU = mybir.AluOpType
AXX = mybir.AxisListType.X
BF = ml_dtypes.bfloat16

B_ALL, S, D = 16, 2048, 2048
NCORES = 8
BPC = B_ALL // NCORES
TOKS = BPC * S
NH = D // 128
H, KV, HD, RD = 16, 4, 128, 64
NR2 = RD // 2
IFF = 8192
NQUAD = 16
WD_GRP = 4
NGRP = 64 // WD_GRP
SCALE = 1.0 / math.sqrt(HD + RD)
NBLK = S // 512
EPS = 1e-5

DEBUG_DUMPS = ()
PHASES = ("p1", "p2", "ffn", "att")

_CACHE = {}



def _ffn_perm():
    q, m, p = np.meshgrid(np.arange(16), np.arange(4), np.arange(128), indexing="ij")
    perm = np.empty(IFF, np.int64)
    perm[(q * 512 + m * 128 + p).ravel()] = (4 * (q * 128 + p) + m).ravel()
    return perm


def _rope_perm(nheads):
    cols = []
    for x2 in (0, 1):
        for h in range(nheads):
            for f in range(NR2):
                cols.append(h * RD + 2 * f + x2)
    return np.asarray(cols, np.int64)


def _make_wg_scr(w):
    return np.ascontiguousarray(w.reshape(16, 128, 16, 512).transpose(2, 1, 0, 3).reshape(2048, 8192))


def _make_wd_scr(w):
    return np.ascontiguousarray(w.reshape(64, 128, 2, 1024).transpose(2, 0, 1, 3).reshape(16384, 1024))


def _host_consts():
    c = {}
    c["identb"] = np.eye(128, dtype=BF)
    c["identf"] = np.eye(128, dtype=np.float32)
    c["onescol"] = np.ones((128, 1), dtype=BF)
    c["onesrow"] = np.ones((1, 128), dtype=np.float32)
    c["onesrowb"] = np.ones((1, 128), dtype=BF)
    freqs = 1.0 / (10000.0 ** (np.arange(0, RD, 2, dtype=np.float64) / RD))
    t = np.arange(S, dtype=np.float64)
    ang = np.outer(freqs, t)
    c["cosb"] = np.tile(np.cos(ang), (4, 1)).astype(BF)
    c["sinb"] = np.tile(np.sin(ang), (4, 1)).astype(BF)
    kk = np.arange(128)[:, None]
    qq = np.arange(512)[None, :]
    c["bmask"] = np.ascontiguousarray(
        np.concatenate([(qq >= kk + d).astype(BF) for d in (0, 128, 256, 384)], axis=1))
    return c


def _host_prep(inputs):
    x = np.asarray(inputs["L0_attn_hidden_states"])
    fperm = _ffn_perm()
    Wo = np.asarray(inputs["L0_attn_Wo"])
    shared = {
        "wq": np.ascontiguousarray(np.asarray(inputs["L0_attn_Wq"])),
        "wqr": np.ascontiguousarray(np.asarray(inputs["L0_attn_Wqrope"])[:, _rope_perm(H)]),
        "wkvc": np.ascontiguousarray(np.asarray(inputs["L0_attn_Wkvc"])),
        "wkdec": np.ascontiguousarray(np.asarray(inputs["L0_attn_Wkdec"])),
        "wvdec": np.ascontiguousarray(np.asarray(inputs["L0_attn_Wvdec"])),
        "wkr": np.ascontiguousarray(np.asarray(inputs["L0_attn_Wkrope"])[:, _rope_perm(KV)]),
        "won": np.ascontiguousarray(Wo.reshape(H, HD + RD, D)[:, :HD].reshape(H * HD, D)),
        "wg32": _make_wg_scr(np.ascontiguousarray(np.asarray(inputs["L0_ffn_w_gate"])[:, fperm])),
        "wu32": _make_wg_scr(np.ascontiguousarray(np.asarray(inputs["L0_ffn_w_up"])[:, fperm])),
        "wd32": _make_wd_scr(np.ascontiguousarray(np.asarray(inputs["L0_ffn_w_down"])[fperm, :])),
    }
    shared.update(_host_consts())
    in_maps = []
    for cid in range(NCORES):
        m = dict(shared)
        m["x"] = np.ascontiguousarray(x[BPC * cid: BPC * (cid + 1)].reshape(TOKS, D))
        in_maps.append(m)
    return in_maps



class _Ctx:
    pass


_TENSORS = [
    ("x", [TOKS, D], F32, "ExternalInput"),
    ("wq", [D, D], F32, "ExternalInput"),
    ("wqr", [D, H * RD], F32, "ExternalInput"),
    ("wkvc", [D, 512], F32, "ExternalInput"),
    ("wkdec", [512, 512], F32, "ExternalInput"),
    ("wvdec", [512, 512], F32, "ExternalInput"),
    ("wkr", [512, KV * RD], F32, "ExternalInput"),
    ("won", [H * HD, D], F32, "ExternalInput"),
    ("wg32", [2048, 8192], F32, "ExternalInput"),
    ("wu32", [2048, 8192], F32, "ExternalInput"),
    ("wd32", [16384, 1024], F32, "ExternalInput"),
    ("identb", [128, 128], BF16, "ExternalInput"),
    ("identf", [128, 128], F32, "ExternalInput"),
    ("onescol", [128, 1], BF16, "ExternalInput"),
    ("onesrow", [1, 128], F32, "ExternalInput"),
    ("onesrowb", [1, 128], BF16, "ExternalInput"),
    ("cosb", [128, S], BF16, "ExternalInput"),
    ("sinb", [128, S], BF16, "ExternalInput"),
    ("bmask", [128, 4 * 512], BF16, "ExternalInput"),
    ("out", [TOKS, D], F32, "ExternalOutput"),
    ("xnT_scr", [BPC * NBLK * 128, NH * 512], BF16, "Internal"),
    ("qn_scr", [BPC * H * 128, S], BF16, "Internal"),
    ("qr_scr", [BPC * H * RD, S], BF16, "Internal"),
    ("kn_scr", [BPC * KV * 128, S], BF16, "Internal"),
    ("kr_scr", [BPC * KV * RD, S], BF16, "Internal"),
    ("v_scr", [BPC * S, 512], BF16, "Internal"),
    ("yf_scr", [TOKS, D], BF16, "Internal"),
    ("wg_scr", [2048, 8192], BF16, "Internal"),
    ("wu_scr", [2048, 8192], BF16, "Internal"),
    ("wd_scr", [16384, 1024], BF16, "Internal"),
]

SEM_NAMES = [
    "x", "xf", "sqa", "ln", "sq2", "ln2", "tp", "tpf", "xnw", "lns",
    "cv", "cvw",
    "aw", "mm", "ev", "sq", "nm", "nr", "rp", "wrt",
    "w", "wf", "g", "sil", "ubc", "hm", "hq", "wd", "wdf", "ye",
    "yo", "xs",
    "kv", "qh", "sc", "ex", "em", "zr", "on", "wo", "yq", "outd", "wol",
    "av", "yw",
]


def _zero_counts():
    return {n: 0 for n in SEM_NAMES}


def _declare(nc, stack):
    C = _Ctx()
    C.t = {}
    for name, shp, dt, kind in _TENSORS:
        if name in DEBUG_DUMPS and kind == "Internal":
            kind = "ExternalOutput"
        C.t[name] = nc.dram_tensor(name, shp, dt, kind=kind).ap()
    C.s = {n: stack.enter_context(nc.semaphore("s_" + n)) for n in SEM_NAMES}

    sb = lambda name, shp, dt: stack.enter_context(nc.sbuf_tensor(name, shp, dt))
    C.identb = sb("identb_sb", [128, 128], BF16).ap()
    C.identf = sb("identf_sb", [128, 128], F32).ap()
    C.onescol = sb("onescol_sb", [128, 1], BF16).ap()
    C.onesrow = sb("onesrow_sb", [1, 128], F32).ap()
    C.cosb = sb("cosb_sb", [128, S], BF16).ap()
    C.sinb = sb("sinb_sb", [128, S], BF16).ap()
    C.bmask = sb("bmask_sb", [128, 4 * 512], BF16).ap()
    C.negB = sb("negB_sb", [128, 2 * H], F32).ap()
    C.mx = sb("mx_sb", [1, 64], F32).ap()
    C.small = sb("small_sb", [128, 32], F32).ap()
    C.rowbuf = sb("rowbuf", [1, 512], BF16).ap()
    C.onesrowb = sb("onesrowb_sb", [1, 128], BF16).ap()
    C.epsap = sb("eps_sb", [128, 1], F32).ap()

    C.blkA = sb("blkA", [128, 32768], BF16).ap()
    C.blkB = sb("blkB", [128, 32768], BF16).ap()
    C.stq = sb("stq", [128, 8192], BF16).ap()
    C.xtile = sb("xtile", [128, 4096], F32).ap()
    C.pool = sb("pool", [128, 15360], BF16).ap()
    return C


def _alloc_overlays(nc, C):
    aA = nc.lookup_mloc(C.blkA.tensor).addr
    aB = nc.lookup_mloc(C.blkB.tensor).addr
    aQ = nc.lookup_mloc(C.stq.tensor).addr
    aX = nc.lookup_mloc(C.xtile.tensor).addr
    aP = nc.lookup_mloc(C.pool.tensor).addr
    at = lambda name, shp, dt, off: nc.alloc_sbuf_tensor_at(name, shp, dt, offset=off).ap()

    C.xnT = at("xnT", [128, NH * S], BF16, aA)
    C.warena = at("warena", [128, 32768], BF16, aB)
    C.stage = at("stage", [128, 2 * 2048], BF16, aQ)
    C.sqs = at("sqs", [128, 2 * 2048], BF16, aQ + 8192)

    C.convb = at("convb", [128, 2 * 4096], BF16, aP)

    C.ctb = at("ctb", [128, 4 * 2048], BF16, aX)
    C.tmp0 = at("tmp0", [128, 2048], BF16, aP + 16384)
    C.tmp1 = at("tmp1", [128, 2048], BF16, aP + 20480)

    C.h_sp = at("h_sp", [128, 64 * 512], BF16, aA)
    C.wgwu = at("wgwu", [128, 2 * 2 * NH * 512], BF16, aB)
    C.xnslot = at("xnslot", [128, NH * 512], BF16, aQ)
    C.wdslot = at("wdslot", [128, 2 * WD_GRP * 1024], BF16, aX)
    C.sg = at("sg", [128, 4 * 512], BF16, aP)
    C.ub = at("ub", [128, 4 * 512], BF16, aP + 4096)
    C.sq4 = at("sq4", [128, 4 * 512], BF16, aP + 8192)
    C.cmp2 = at("cmp2", [128, 2 * 512], BF16, aP + 12288)
    C.ystage = at("ystage", [128, 4 * 2048], BF16, aP + 14336)

    C.obuf = at("obuf", [128, H * S], BF16, aB)
    off = aA
    def ata(name, shp, dt):
        nonlocal off
        ap = at(name, shp, dt, off)
        nbytes = int(np.prod(shp[1:])) * (2 if dt == BF16 else 4)
        off += (nbytes + 31) & ~31
        return ap
    C.vsb = ata("vsb", [128, 16 * 512], BF16)
    C.karena = ata("karena", [128, 2 * 2048], BF16)
    C.krarena = ata("krarena", [128, 2 * 2048], BF16)
    C.qn_a = ata("qn_a", [128, 2 * 2048], BF16)
    C.qr_a = ata("qr_a", [128, 2 * 2048], BF16)
    C.esb = ata("esb", [128, 2 * 512], BF16)
    C.rzbuf = ata("rzbuf", [128, 8], F32)
    C.rbsb = ata("rbsb", [128, 512], F32)
    assert off - aA <= 53 * 1024

    C.wosb = at("wosb", [128, H * 2048], BF16, aA)
    C.yfslot = at("yfslot", [128, 2 * 2048], BF16, aP)
    C.ostage = at("ostage", [128, 2 * 2048], F32, aP + 8192)

    C.pA = nc.alloc_psum_tensor("pA", [128, 2048], F32).ap()
    C.pB = nc.alloc_psum_tensor("pB", [128, 2048], F32).ap()
    saved = nc.psum_base
    nc.psum_base = 0
    C.ptp = nc.alloc_psum_tensor("ptp", [128, 2 * 2048], BF16).ap()
    nc.psum_base = 0
    C.pS = nc.alloc_psum_tensor("pS", [128, 2 * 512], F32).ap()
    C.pO = nc.alloc_psum_tensor("pO", [128, 2 * 512], F32).ap()
    C.pZ = nc.alloc_psum_tensor("pZ", [128, 512], F32).ap()
    C.pR = nc.alloc_psum_tensor("pR", [1, 512], F32).ap()
    C.pBC = nc.alloc_psum_tensor("pBC", [128, 512], F32).ap()
    nc.psum_base = saved
    return C


def _gate(e, C, b, names):
    for n in names:
        if b[n] > 0:
            e.wait_ge(C.s[n], b[n])


GATE_P1 = ("wo", "yq", "outd", "wol", "wdf", "hq", "ye", "yo", "xs", "wrt", "nr")
GATE_P2 = ("tp", "tpf", "sq2", "xnw")
GATE_FFN = ("mm", "ev", "sq", "nr", "wrt", "aw", "cvw")
GATE_ATTA = ("wdf", "hq", "ye", "w", "wd", "yo", "xs")
GATE_ATTB = ("on", "ex", "em", "kv", "qh", "zr")



def _emit_p1(eng, e, C, b, batch):
    s = C.s
    t = C.t
    nsub = 16
    _gate(e, C, b, GATE_P1)
    if eng == "sync":
        for i in range(nsub):
            slot = i % 2
            prev = b["ln2"] + i - 2
            if prev >= 0:
                e.wait_ge(s["ln2"], prev + 1)
            e.dma_start(C.xtile[:, slot * 2048:(slot + 1) * 2048],
                        t["x"][batch * S + i * 128: batch * S + (i + 1) * 128, :]
                        ).then_inc(s["x"], 16)
        for blk in range(NBLK):
            e.wait_ge(s["tpf"], b["tpf"] + (blk + 1) * 4)
            rows = (batch * NBLK + blk) * 128
            e.dma_start(t["xnT_scr"][rows: rows + 128, :].rearrange("p (h n) -> p h n", h=NH),
                        C.xnT.rearrange("p (h t) -> p h t", h=NH)[:, :, blk * 512:(blk + 1) * 512]
                        ).then_inc(s["xnw"], 16)

    if eng == "scalar":
        for i in range(nsub):
            slot = i % 2
            sm = C.small[:, slot * 8: slot * 8 + 8]
            e.wait_ge(s["ln"], b["ln"] + i + 1)
            e.activation(sm[:, 4:5], sm[:, 3:4], AF.Sqrt, bias=C.epsap).then_inc(s["sq2"], 1)

    if eng == "vector":
        if b["xnw"] > 0:
            e.wait_ge(s["xnw"], b["xnw"])

        def evac(j):
            e.wait_ge(s["tp"], b["tp"] + j + 1)
            e.tensor_copy(
                C.xnT.rearrange("p (h t) -> p h t", h=NH)[:, :, j * 128:(j + 1) * 128],
                C.ptp[:, (j % 2) * 2048:((j % 2) + 1) * 2048].rearrange("p (h t) -> p h t", h=NH)
            ).then_inc(s["tpf"], 1)

        for i in range(nsub):
            slot = i % 2
            xap = C.xtile[:, slot * 2048:(slot + 1) * 2048]
            sm = C.small[:, slot * 8: slot * 8 + 8]
            e.wait_ge(s["x"], b["x"] + (i + 1) * 16)
            e.reduce_sum(sm[:, 1:2], xap, AXX).then_inc(s["lns"], 1)
            e.tensor_mul(C.sqs[:, slot * 2048:(slot + 1) * 2048], xap, xap)
            e.reduce_sum(sm[:, 0:1], C.sqs[:, slot * 2048:(slot + 1) * 2048], AXX
                         ).then_inc(s["lns"], 1)
            e.wait_ge(s["lns"], b["lns"] + 5 * i + 2)
            e.tensor_scalar(sm[:, 2:3], sm[:, 1:2], 1.0 / D, None, ALU.mult
                            ).then_inc(s["lns"], 1)
            e.wait_ge(s["lns"], b["lns"] + 5 * i + 3)
            e.tensor_mul(sm[:, 5:6], sm[:, 2:3], sm[:, 2:3]).then_inc(s["lns"], 1)
            e.wait_ge(s["lns"], b["lns"] + 5 * i + 4)
            e.tensor_scalar(sm[:, 3:4], sm[:, 0:1], 1.0 / D, sm[:, 5:6],
                            ALU.mult, ALU.subtract).then_inc(s["ln"], 1)
            e.wait_ge(s["sq2"], b["sq2"] + i + 1)
            e.reciprocal(sm[:, 6:7], sm[:, 4:5]).then_inc(s["lns"], 1)
            if b["tp"] + i - 1 >= 1:
                e.wait_ge(s["tp"], b["tp"] + i - 1)
            e.wait_ge(s["lns"], b["lns"] + 5 * i + 5)
            e.tensor_scalar(C.stage[:, slot * 2048:(slot + 1) * 2048], xap,
                            sm[:, 2:3], sm[:, 6:7], ALU.subtract, ALU.mult
                            ).then_inc(s["ln2"], 1)
            if i >= 1:
                evac(i - 1)
        evac(nsub - 1)

    if eng == "tensor":
        for i in range(nsub):
            slot = i % 2
            e.wait_ge(s["ln2"], b["ln2"] + i + 1)
            if b["tpf"] + i - 1 >= 1:
                e.wait_ge(s["tpf"], b["tpf"] + i - 1)
            for hc in range(NH):
                ins = e.transpose(C.ptp[:, slot * 2048 + hc * 128: slot * 2048 + (hc + 1) * 128],
                                  C.stage[:, slot * 2048 + hc * 128: slot * 2048 + (hc + 1) * 128],
                                  C.identb)
            ins.then_inc(s["tp"], 1)

    for k, d in (("x", 256), ("ln", 16), ("sq2", 16), ("lns", 5 * 16),
                 ("ln2", 16), ("tp", 16), ("tpf", 16), ("xnw", 16 * NBLK)):
        b[k] += d
    return b



def _p2_chunks():
    out = []
    for pr in range(4):
        out.append(("qr1", pr))
        out.append(("qr2", pr))
    for c in range(4):
        out.append(("c", c))
    for g in range(KV):
        out.append(("kn", g))
    out.append(("kr1", 0))
    out.append(("kr2", 0))
    for kt in range(16):
        out.append(("v", kt))
    for h in range(H):
        out.append(("qn", h))
    return out


_PAIR1 = ("qr1", "kr1")
_PAIR2 = ("qr2", "kr2")
_NORMED = {"qn": True, "qr1": True, "qr2": True, "c": False, "kn": True,
           "kr1": True, "kr2": True, "v": False}


def _p2_sched():
    chunks = _p2_chunks()
    nC = len(chunks)
    mm_at, ev_at, sq_at, nm_at, nr_at, wrt_at = {}, {}, {}, {}, {}, {}
    psum_name, stage_slot, nst_at = {}, {}, {}
    cmm = cev = csq = cnm = cwrt = 0
    nbig = 0
    nst = 0
    wrt_of = {"qn": 1, "qr1": 0, "qr2": 8, "c": 0, "kn": 1, "kr1": 0, "kr2": 8, "v": 1}
    for i, (kind, idx) in enumerate(chunks):
        cmm += 1
        mm_at[i] = cmm
        cev += 1
        ev_at[i] = cev
        if _NORMED[kind]:
            csq += 1
            cnm += 1
        sq_at[i] = csq
        nm_at[i] = cnm
        nr_at[i] = cnm
        cwrt += wrt_of[kind]
        wrt_at[i] = cwrt
        if kind in _PAIR1:
            psum_name[i] = "A"
        elif kind in _PAIR2:
            psum_name[i] = "B"
        elif kind == "v":
            psum_name[i] = "A" if (idx // 4) % 2 == 0 else "B"
        else:
            psum_name[i] = "A" if nbig % 2 == 0 else "B"
            nbig += 1
        if kind in ("qn", "kn"):
            stage_slot[i] = nst % 2
            nst_at[i] = nst
            nst += 1
    tot = dict(mm=cmm, ev=cev, sq=csq, nm=cnm, nr=cnm, wrt=cwrt)
    return chunks, dict(mm_at=mm_at, ev_at=ev_at, sq_at=sq_at, nm_at=nm_at,
                        nr_at=nr_at, wrt_at=wrt_at, psum_name=psum_name,
                        stage_slot=stage_slot, nst_at=nst_at, tot=tot)


_P2C, _P2S = _p2_sched()


def _emit_p2(eng, e, C, b, batch):
    s = C.s
    t = C.t
    _gate(e, C, b, GATE_P2)
    chunks, sch = _P2C, _P2S
    nC = len(chunks)
    tot = sch["tot"]
    mm_at, ev_at, sq_at = sch["mm_at"], sch["ev_at"], sch["sq_at"]
    nm_at, nr_at, wrt_at = sch["nm_at"], sch["nr_at"], sch["wrt_at"]

    def psum_of(i):
        base = C.pA if sch["psum_name"][i] == "A" else C.pB
        kind, idx = chunks[i]
        if kind == "v":
            return base[:, (idx % 4) * 512:((idx % 4) + 1) * 512]
        return base

    def prev_user(i):
        for j in range(i - 1, -1, -1):
            if sch["psum_name"][j] == sch["psum_name"][i]:
                return j
        return None

    def stage_of(i):
        kind = chunks[i][0]
        if kind in _PAIR1:
            return C.stage[:, 0:2048]
        if kind in _PAIR2:
            return C.stage[:, 2048:4096]
        sl = sch["stage_slot"][i]
        return C.stage[:, sl * 2048:(sl + 1) * 2048]

    def vstage(kt):
        off = (kt % 8) * 512
        return C.stage[:, off: off + 512]

    def sq_slot(i):
        nsq = sq_at[i]
        return C.sqs[:, ((nsq - 1) % 2) * 2048:(((nsq - 1) % 2) + 1) * 2048]

    def stage_prev(i):
        nst = sch["nst_at"][i]
        if nst < 2:
            return None
        for j in range(i):
            if j in sch["nst_at"] and sch["nst_at"][j] == nst - 2:
                return j
        return None

    def main_mms(kind, idx, X):
        ins = None
        if kind == "qn":
            for hc in range(NH):
                for tb in range(4):
                    ins = e.matmul(X[:, tb * 512:(tb + 1) * 512],
                                   C.warena[:, hc * 2048 + idx * 128: hc * 2048 + idx * 128 + 128],
                                   C.xnT[:, hc * 2048 + tb * 512: hc * 2048 + (tb + 1) * 512],
                                   start=(hc == 0), stop=(hc == NH - 1))
        elif kind in ("qr1", "qr2"):
            cc = idx if kind == "qr1" else 4 + idx
            for hc in range(NH):
                for tb in range(4):
                    ins = e.matmul(X[:, tb * 512:(tb + 1) * 512],
                                   C.warena[:, hc * 1024 + cc * 128: hc * 1024 + cc * 128 + 128],
                                   C.xnT[:, hc * 2048 + tb * 512: hc * 2048 + (tb + 1) * 512],
                                   start=(hc == 0), stop=(hc == NH - 1))
        elif kind == "c":
            for hc in range(NH):
                for tb in range(4):
                    ins = e.matmul(X[:, tb * 512:(tb + 1) * 512],
                                   C.warena[:, 16384 + hc * 512 + idx * 128: 16384 + hc * 512 + idx * 128 + 128],
                                   C.xnT[:, hc * 2048 + tb * 512: hc * 2048 + (tb + 1) * 512],
                                   start=(hc == 0), stop=(hc == NH - 1))
        elif kind == "kn":
            for cc in range(4):
                for tb in range(4):
                    ins = e.matmul(X[:, tb * 512:(tb + 1) * 512],
                                   C.warena[:, 24576 + cc * 512 + idx * 128: 24576 + cc * 512 + idx * 128 + 128],
                                   C.ctb[:, cc * 2048 + tb * 512: cc * 2048 + (tb + 1) * 512],
                                   start=(cc == 0), stop=(cc == 3))
        elif kind in ("kr1", "kr2"):
            x0 = 0 if kind == "kr1" else 128
            for cc in range(4):
                for tb in range(4):
                    ins = e.matmul(X[:, tb * 512:(tb + 1) * 512],
                                   C.warena[:, 28672 + cc * 256 + x0: 28672 + cc * 256 + x0 + 128],
                                   C.ctb[:, cc * 2048 + tb * 512: cc * 2048 + (tb + 1) * 512],
                                   start=(cc == 0), stop=(cc == 3))
        elif kind == "v":
            for cc in range(4):
                ins = e.matmul(X,
                               C.ctb[:, cc * 2048 + idx * 128: cc * 2048 + idx * 128 + 128],
                               C.warena[:, 26624 + cc * 512: 26624 + (cc + 1) * 512],
                               start=(cc == 0), stop=(cc == 3))
        return ins

    if eng == "gpsimd":
        for hc in range(NH):
            e.dma_start(C.warena[:, hc * 1024:(hc + 1) * 1024],
                        t["wqr"][hc * 128:(hc + 1) * 128, :]).then_inc(s["aw"], 16)
        for hc in range(NH):
            e.dma_start(C.warena[:, 16384 + hc * 512: 16384 + (hc + 1) * 512],
                        t["wkvc"][hc * 128:(hc + 1) * 128, :]).then_inc(s["aw"], 16)
        for cc in range(4):
            e.dma_start(C.warena[:, 24576 + cc * 512: 24576 + (cc + 1) * 512],
                        t["wkdec"][cc * 128:(cc + 1) * 128, :]).then_inc(s["aw"], 16)
        for cc in range(4):
            e.dma_start(C.warena[:, 26624 + cc * 512: 26624 + (cc + 1) * 512],
                        t["wvdec"][cc * 128:(cc + 1) * 128, :]).then_inc(s["aw"], 16)
        for cc in range(4):
            e.dma_start(C.warena[:, 28672 + cc * 256: 28672 + (cc + 1) * 256],
                        t["wkr"][cc * 128:(cc + 1) * 128, :]).then_inc(s["aw"], 16)
        e.wait_ge(s["mm"], b["mm"] + 8)
        for hc in range(8):
            e.dma_start(C.warena[:, hc * 2048:(hc + 1) * 2048],
                        t["wq"][hc * 128:(hc + 1) * 128, :]).then_inc(s["aw"], 16)
        e.wait_ge(s["mm"], b["mm"] + 34)
        for hc in range(8, NH):
            e.dma_start(C.warena[:, hc * 2048:(hc + 1) * 2048],
                        t["wq"][hc * 128:(hc + 1) * 128, :]).then_inc(s["aw"], 16)
        if batch == 0:
            _emit_conversion(e, C, b)

    if eng == "tensor":
        AW_SMALL = b["aw"] + 16 * (16 + 16 + 4 + 4 + 4)
        AW_ALL = b["aw"] + 16 * (16 + 16 + 16 + 4 + 4 + 4)
        done_norm = set()

        def norms(j):
            kind = chunks[j][0]
            X = C.pA if sch["psum_name"][j] == "A" else C.pB
            e.wait_ge(s["sq"], b["sq"] + sq_at[j])
            e.wait_ge(s["ev"], b["ev"] + ev_at[j])
            sqap = sq_slot(j)
            for tb in range(4):
                ins = e.matmul(X[0:1, tb * 512:(tb + 1) * 512], C.onescol,
                               sqap[:, tb * 512:(tb + 1) * 512], start=True, stop=True)
            ins.then_inc(s["nm"], 1)
            done_norm.add(j)

        for i, (kind, idx) in enumerate(chunks):
            e.wait_ge(s["aw"], AW_ALL if kind == "qn" else AW_SMALL)
            pu = prev_user(i)
            if pu is not None:
                e.wait_ge(s["ev"], b["ev"] + ev_at[pu])
                if _NORMED[chunks[pu][0]]:
                    e.wait_ge(s["nr"], b["nr"] + nr_at[pu])
            ins = main_mms(kind, idx, psum_of(i))
            ins.then_inc(s["mm"], 1)
            for j in range(i):
                if _NORMED[chunks[j][0]] and j not in done_norm:
                    norms(j)
        for j in range(nC):
            if _NORMED[chunks[j][0]] and j not in done_norm:
                norms(j)
        _p2_negb(eng, e, C, b, batch, tot)

    if eng == "vector":
        npair2 = 0
        for i, (kind, idx) in enumerate(chunks):
            X = psum_of(i)
            e.wait_ge(s["mm"], b["mm"] + mm_at[i])
            if kind == "v":
                if i >= 2:
                    e.wait_ge(s["wrt"], b["wrt"] + 16 * wrt_at[i - 2])
                    e.wait_ge(s["sq"], b["sq"] + sq_at[i - 2])
                e.tensor_copy(vstage(idx), X).then_inc(s["ev"], 1)
            elif kind == "c":
                e.tensor_copy(C.ctb[:, idx * 2048:(idx + 1) * 2048], X).then_inc(s["ev"], 1)
            elif kind in ("qn", "kn"):
                if i >= 2:
                    e.wait_ge(s["wrt"], b["wrt"] + 16 * wrt_at[i - 2])
                    e.wait_ge(s["sq"], b["sq"] + sq_at[i - 2])
                e.tensor_copy(stage_of(i), X).then_inc(s["ev"], 1)
            elif kind in _PAIR1:
                if npair2 >= 1:
                    e.wait_ge(s["wrt"], b["wrt"] + 16 * wrt_at[i - 1])
                    e.wait_ge(s["sq"], b["sq"] + sq_at[i - 1])
                e.tensor_copy(stage_of(i), X).then_inc(s["ev"], 1)
            else:
                e.tensor_copy(stage_of(i), X).then_inc(s["ev"], 1)
                st0 = C.stage[:, 0:2048]
                st1 = C.stage[:, 2048:4096]
                e.wait_ge(s["sq"], b["sq"] + sq_at[i])
                e.tensor_mul(C.tmp0, st0, C.cosb)
                e.tensor_mul(C.tmp1, st1, C.sinb)
                e.tensor_sub(C.tmp0, C.tmp0, C.tmp1)
                e.tensor_mul(C.tmp1, st0, C.sinb)
                e.tensor_mul(st1, st1, C.cosb)
                e.tensor_add(st1, st1, C.tmp1).then_inc(s["rp"], 1)
                npair2 += 1
            if _NORMED[kind]:
                e.wait_ge(s["nm"], b["nm"] + nm_at[i])
                mxcol = {"qn": idx, "qr1": 16 + idx, "qr2": 20 + idx,
                         "kn": 24 + idx, "kr1": 28, "kr2": 29}[kind]
                XB = C.pA if sch["psum_name"][i] == "A" else C.pB
                e.reduce_max(C.mx[0:1, mxcol:mxcol + 1], XB[0:1, 0:2048], AXX
                             ).then_inc(s["nr"], 1)
        _p2_negb(eng, e, C, b, batch, tot)

    if eng == "scalar":
        jn = 0
        for i, (kind, idx) in enumerate(chunks):
            if not _NORMED[kind]:
                continue
            jn += 1
            if jn >= 3:
                e.wait_ge(s["nm"], b["nm"] + jn - 2)
            e.wait_ge(s["ev"], b["ev"] + ev_at[i])
            e.activation(sq_slot(i), stage_of(i), AF.Square).then_inc(s["sq"], 1)
        _p2_negb(eng, e, C, b, batch, tot)

    if eng == "sync":
        npair2 = 0
        for i, (kind, idx) in enumerate(chunks):
            if kind == "qn":
                e.wait_ge(s["ev"], b["ev"] + ev_at[i])
                rows = (batch * H + idx) * 128
                e.dma_start(t["qn_scr"][rows: rows + 128, :], stage_of(i)).then_inc(s["wrt"], 16)
            elif kind == "kn":
                e.wait_ge(s["ev"], b["ev"] + ev_at[i])
                rows = (batch * KV + idx) * 128
                e.dma_start(t["kn_scr"][rows: rows + 128, :], stage_of(i)).then_inc(s["wrt"], 16)
            elif kind == "v":
                e.wait_ge(s["ev"], b["ev"] + ev_at[i])
                rows = (batch * 16 + idx) * 128
                e.dma_start(t["v_scr"][rows: rows + 128, :], vstage(idx)).then_inc(s["wrt"], 16)
            elif kind in _PAIR2:
                npair2 += 1
                e.wait_ge(s["rp"], b["rp"] + npair2)
                if kind == "qr2":
                    for hh in range(4):
                        h = idx * 4 + hh
                        rows = (batch * H + h) * RD
                        e.dma_start(t["qr_scr"][rows: rows + 32, :],
                                    C.tmp0[hh * 32:(hh + 1) * 32, :]).then_inc(s["wrt"], 16)
                        e.dma_start(t["qr_scr"][rows + 32: rows + 64, :],
                                    C.stage[hh * 32:(hh + 1) * 32, 2048:4096]).then_inc(s["wrt"], 16)
                else:
                    for g in range(KV):
                        rows = (batch * KV + g) * RD
                        e.dma_start(t["kr_scr"][rows: rows + 32, :],
                                    C.tmp0[g * 32:(g + 1) * 32, :]).then_inc(s["wrt"], 16)
                        e.dma_start(t["kr_scr"][rows + 32: rows + 64, :],
                                    C.stage[g * 32:(g + 1) * 32, 2048:4096]).then_inc(s["wrt"], 16)
        _p2_negb(eng, e, C, b, batch, tot)

    b["aw"] += 16 * (16 + 16 + 16 + 4 + 4 + 4)
    b["mm"] += tot["mm"]
    b["ev"] += tot["ev"]
    b["sq"] += tot["sq"]
    b["nm"] += tot["nm"] + 1
    b["nr"] += tot["nr"] + 1
    b["rp"] += 5 + 15
    b["wrt"] += 16 * tot["wrt"]
    if batch == 0:
        b["cv"] += CONV_CV
        b["cvw"] += CONV_CVW
    return b


def _p2_negb(eng, e, C, b, batch, tot):
    s = C.s
    rp0 = b["rp"] + 5
    if eng == "vector":
        e.wait_ge(s["nr"], b["nr"] + tot["nr"])
        k = rp0

        def step(f):
            nonlocal k
            e.wait_ge(s["rp"], k)
            f().then_inc(s["rp"], 1)
            k += 1

        for c in range(4):
            step(lambda c=c: e.tensor_scalar(C.mx[0:1, 32 + 4 * c: 36 + 4 * c],
                                             C.mx[0:1, 4 * c: 4 * c + 4],
                                             C.mx[0:1, 16 + c: 17 + c], None, ALU.add))
        for c in range(4):
            step(lambda c=c: e.tensor_scalar(C.mx[0:1, 32 + 4 * c: 36 + 4 * c],
                                             C.mx[0:1, 32 + 4 * c: 36 + 4 * c],
                                             C.mx[0:1, 20 + c: 21 + c], None, ALU.add))
        step(lambda: e.tensor_scalar(C.mx[0:1, 48:52], C.mx[0:1, 24:28],
                                     C.mx[0:1, 28:29], None, ALU.add))
        step(lambda: e.tensor_scalar(C.mx[0:1, 48:52], C.mx[0:1, 48:52],
                                     C.mx[0:1, 29:30], None, ALU.add))
        for c in range(4):
            step(lambda c=c: e.tensor_scalar(C.mx[0:1, 32 + 4 * c: 36 + 4 * c],
                                             C.mx[0:1, 32 + 4 * c: 36 + 4 * c],
                                             C.mx[0:1, 48 + c: 49 + c], None, ALU.mult))
        e.wait_ge(s["rp"], k + 1)
        e.wait_ge(s["nm"], b["nm"] + tot["nm"] + 1)
        e.tensor_copy(C.negB[:, batch * H:(batch + 1) * H], C.pA[:, 0:16]
                      ).then_inc(s["nr"], 1)
    if eng == "scalar":
        e.wait_ge(s["rp"], rp0 + 14)
        e.activation(C.mx[0:1, 48:64], C.mx[0:1, 32:48], AF.Sqrt)
        e.mul(C.mx[0:1, 48:64], C.mx[0:1, 48:64], -SCALE).then_inc(s["rp"], 1)
    if eng == "tensor":
        e.wait_ge(s["rp"], rp0 + 15)
        e.matmul(C.pA[:, 0:16], C.onesrow, C.mx[0:1, 48:64], start=True, stop=True
                 ).then_inc(s["nm"], 1)


CONV_CV = 96 * 16
CONV_CVW = 96 * 16


def _emit_conversion(e, C, b):
    s = C.s
    t = C.t
    k = 0
    for (src, dst) in (("wg32", "wg_scr"), ("wu32", "wu_scr")):
        for rb in range(16):
            for half in range(2):
                slot = k % 2
                if k >= 2:
                    e.wait_ge(s["cvw"], b["cvw"] + 16 * (k - 1))
                e.dma_start(C.convb[:, slot * 4096:(slot + 1) * 4096],
                            t[src][rb * 128:(rb + 1) * 128, half * 4096:(half + 1) * 4096]
                            ).then_inc(s["cv"], 16)
                e.wait_ge(s["cv"], b["cv"] + 16 * (k + 1))
                e.dma_start(t[dst][rb * 128:(rb + 1) * 128, half * 4096:(half + 1) * 4096],
                            C.convb[:, slot * 4096:(slot + 1) * 4096]).then_inc(s["cvw"], 16)
                k += 1
    for blk in range(32):
        slot = k % 2
        e.wait_ge(s["cvw"], b["cvw"] + 16 * (k - 1))
        e.dma_start(C.convb[:, slot * 4096:(slot + 1) * 4096],
                    t["wd32"][blk * 512:(blk + 1) * 512, :].rearrange("(p c) n -> p (c n)", c=4)
                    ).then_inc(s["cv"], 16)
        e.wait_ge(s["cv"], b["cv"] + 16 * (k + 1))
        e.dma_start(t["wd_scr"][blk * 512:(blk + 1) * 512, :].rearrange("(p c) n -> p (c n)", c=4),
                    C.convb[:, slot * 4096:(slot + 1) * 4096]).then_inc(s["cvw"], 16)
        k += 1



def _emit_ffn(eng, e, C, b, batch):
    s = C.s
    t = C.t
    _gate(e, C, b, GATE_FFN)
    for blk in range(NBLK):
        _emit_ffn_block(eng, e, C, b, batch, blk)
        for k, d in (("w", 32 * NQUAD), ("wf", NQUAD), ("g", NQUAD), ("sil", NQUAD),
                     ("ubc", NQUAD), ("hm", NQUAD), ("hq", NQUAD),
                     ("wd", 16 * 2 * NGRP), ("wdf", 2 * NGRP), ("ye", 2),
                     ("xs", 16), ("yo", 16 * 4)):
            b[k] += d
    return b


def _emit_ffn_block(eng, e, C, b, batch, blk):
    s = C.s
    t = C.t
    pg, pu = C.pA, C.pB
    first = (batch == 0 and blk == 0)

    if eng == "gpsimd":
        if not first:
            e.wait_ge(s["wf"], b["wf"])
        rows = (batch * NBLK + blk) * 128
        e.dma_start(C.xnslot, t["xnT_scr"][rows: rows + 128, :]).then_inc(s["xs"], 16)
        e.wait_ge(s["ye"], b["ye"] + 2)
        for tokb in range(4):
            rows = (batch * 16 + blk * 4 + tokb) * 128
            e.dma_start(t["yf_scr"][rows: rows + 128, :],
                        C.ystage[:, tokb * 2048:(tokb + 1) * 2048]).then_inc(s["yo"], 16)

    if eng == "sync":
        for q in range(NQUAD):
            slot = q % 2
            prev_user = b["wf"] + q - 2
            if prev_user >= 0:
                e.wait_ge(s["wf"], prev_user + 1)
            cbase = slot * 2 * NH * 512
            e.dma_start(C.wgwu[:, cbase: cbase + NH * 512],
                        t["wg_scr"][q * 128:(q + 1) * 128, :]).then_inc(s["w"], 16)
            e.dma_start(C.wgwu[:, cbase + NH * 512: cbase + 2 * NH * 512],
                        t["wu_scr"][q * 128:(q + 1) * 128, :]).then_inc(s["w"], 16)
        for pi in range(2):
            for grp in range(NGRP):
                i = pi * NGRP + grp
                slot = i % 2
                prev_user = b["wdf"] + i - 2
                if prev_user >= 0:
                    e.wait_ge(s["wdf"], prev_user + 1)
                r0 = (pi * 64 + grp * WD_GRP) * 128
                e.dma_start(C.wdslot[:, slot * WD_GRP * 1024:(slot + 1) * WD_GRP * 1024]
                            .rearrange("p (c n) -> p c n", c=WD_GRP),
                            t["wd_scr"][r0: r0 + WD_GRP * 128, :].rearrange("(c p) n -> p c n", p=128)
                            ).then_inc(s["wd"], 16)

    if eng == "tensor":
        e.wait_ge(s["xs"], b["xs"] + 16)
        for q in range(NQUAD):
            slot = q % 2
            cbase = slot * 2 * NH * 512
            e.wait_ge(s["w"], b["w"] + 32 * (q + 1))
            if b["sil"] + q >= 1:
                e.wait_ge(s["sil"], b["sil"] + q)
            if q == 0 and b["ye"] >= 2:
                e.wait_ge(s["ye"], b["ye"])
            for mc in range(4):
                for hc in range(NH):
                    ins = e.matmul(pg[:, mc * 512:(mc + 1) * 512],
                                   C.wgwu[:, cbase + hc * 512 + mc * 128: cbase + hc * 512 + mc * 128 + 128],
                                   C.xnslot[:, hc * 512:(hc + 1) * 512],
                                   start=(hc == 0), stop=(hc == NH - 1))
            ins.then_inc(s["g"], 1)
            if b["ubc"] + q >= 1:
                e.wait_ge(s["ubc"], b["ubc"] + q)
            for mc in range(4):
                for hc in range(NH):
                    ins = e.matmul(pu[:, mc * 512:(mc + 1) * 512],
                                   C.wgwu[:, cbase + NH * 512 + hc * 512 + mc * 128: cbase + NH * 512 + hc * 512 + mc * 128 + 128],
                                   C.xnslot[:, hc * 512:(hc + 1) * 512],
                                   start=(hc == 0), stop=(hc == NH - 1))
            ins.then_inc(s["wf"], 1)
        e.wait_ge(s["hq"], b["hq"] + NQUAD)
        e.wait_ge(s["sil"], b["sil"] + NQUAD)
        e.wait_ge(s["ubc"], b["ubc"] + NQUAD)
        for pi in range(2):
            if pi == 1:
                e.wait_ge(s["ye"], b["ye"] + 1)
            for grp in range(NGRP):
                i = pi * NGRP + grp
                slot = i % 2
                e.wait_ge(s["wd"], b["wd"] + 16 * (i + 1))
                for ci in range(WD_GRP):
                    Ic = grp * WD_GRP + ci
                    for tokb in range(4):
                        for hgi in range(2):
                            pt = pg if tokb < 2 else pu
                            bank = (tokb % 2) * 2 + hgi
                            ins = e.matmul(
                                pt[:, bank * 512:(bank + 1) * 512],
                                C.h_sp[:, Ic * 512 + tokb * 128: Ic * 512 + tokb * 128 + 128],
                                C.wdslot[:, slot * WD_GRP * 1024 + ci * 1024 + hgi * 512:
                                         slot * WD_GRP * 1024 + ci * 1024 + hgi * 512 + 512],
                                start=(Ic == 0), stop=(Ic == 63))
                ins.then_inc(s["wdf"], 1)

    if eng == "scalar":
        for q in range(NQUAD):
            e.wait_ge(s["g"], b["g"] + q + 1)
            if b["hm"] + q >= 1:
                e.wait_ge(s["hm"], b["hm"] + q)
            for mc in range(4):
                ins = e.activation(C.sg[:, mc * 512:(mc + 1) * 512],
                                   pg[:, mc * 512:(mc + 1) * 512], AF.Silu)
            ins.then_inc(s["sil"], 1)
        for pi in range(2):
            e.wait_ge(s["wdf"], b["wdf"] + (pi + 1) * NGRP)
            if pi == 0 and b["yo"] > 0:
                e.wait_ge(s["yo"], b["yo"])
            for tokb in range(4):
                for hgi in range(2):
                    pt = pg if tokb < 2 else pu
                    bank = (tokb % 2) * 2 + hgi
                    hg = pi * 2 + hgi
                    ins = e.copy(C.ystage[:, tokb * 2048 + hg * 512: tokb * 2048 + hg * 512 + 512],
                                 pt[:, bank * 512:(bank + 1) * 512])
            ins.then_inc(s["ye"], 1)

    if eng == "vector":
        for q in range(NQUAD):
            if q == 0 and b["wdf"] >= 1:
                e.wait_ge(s["wdf"], b["wdf"])
            e.wait_ge(s["wf"], b["wf"] + q + 1)
            for mc in range(4):
                ins = e.tensor_copy(C.ub[:, mc * 512:(mc + 1) * 512],
                                    pu[:, mc * 512:(mc + 1) * 512])
            ins.then_inc(s["ubc"], 1)
            e.wait_ge(s["sil"], b["sil"] + q + 1)
            hs = [C.h_sp[:, (q * 4 + m) * 512:(q * 4 + m + 1) * 512] for m in range(4)]
            for m in range(4):
                ins = e.tensor_mul(hs[m], C.sg[:, m * 512:(m + 1) * 512],
                                   C.ub[:, m * 512:(m + 1) * 512])
            ins.then_inc(s["hm"], 1)
            sqs = [C.sq4[:, m * 512:(m + 1) * 512] for m in range(4)]
            for m in range(4):
                e.tensor_mul(sqs[m], hs[m], hs[m])
            cmp0 = C.cmp2[:, 0:512]
            cmp1 = C.cmp2[:, 512:1024]
            wins = [C.ub[:, m * 512:(m + 1) * 512] for m in range(4)]
            for m in range(4):
                ops = []
                for j in range(4):
                    if j == m:
                        continue
                    op = ALU.is_ge if m < j else ALU.is_gt
                    ops.append((j, op))
                e.tensor_tensor(cmp0, sqs[m], sqs[ops[0][0]], ops[0][1])
                e.tensor_tensor(cmp1, sqs[m], sqs[ops[1][0]], ops[1][1])
                e.tensor_add(wins[m], cmp0, cmp1)
                e.tensor_tensor(cmp0, sqs[m], sqs[ops[2][0]], ops[2][1])
                e.tensor_add(wins[m], wins[m], cmp0)
            for m in range(4):
                ins = e.scalar_tensor_tensor(hs[m], wins[m], 1.5, hs[m], ALU.is_ge, ALU.mult)
            ins.then_inc(s["hq"], 1)



def _atta_sched():
    tiles = []
    head_end = {}
    group_end = {}
    hqb_last = {}
    g = 0
    for h in range(H):
        for qb in range(4):
            for kt in range(4 * qb + 4):
                masked = kt >= 4 * qb
                tiles.append((h, qb, kt, masked, g))
                hqb_last[(h, qb)] = g
                g += 1
        head_end[h] = g
        if h % 4 == 3:
            group_end[h // 4] = g
    em_at = {}
    av_at = {}
    cem = 0
    for (h, qb, kt, masked, gi) in tiles:
        if masked:
            cem += 1
        em_at[gi] = cem
        av_at[gi] = gi + 1
    return tiles, head_end, group_end, hqb_last, em_at


_ATILES, _AHEND, _AGEND, _AHQBL, _AEM = _atta_sched()
NTILE = len(_ATILES)
NMASK = _AEM[_ATILES[-1][4]]


def _emit_atta(eng, e, C, b, batch):
    s = C.s
    t = C.t
    _gate(e, C, b, GATE_ATTA)
    tiles = _ATILES

    if eng == "sync":
        for j in range(4):
            e.dma_start(C.vsb[:, j * 2048:(j + 1) * 2048].rearrange("p (k n) -> p k n", k=4),
                        t["v_scr"][batch * S + j * 512: batch * S + (j + 1) * 512, :]
                        .rearrange("(k p) n -> p k n", p=128)).then_inc(s["kv"], 16)
        for h in range(H):
            g = h // 4
            if h % 4 == 0:
                if g >= 2:
                    e.wait_ge(s["sc"], b["sc"] + _AGEND[g - 2])
                rows = (batch * KV + g) * 128
                e.dma_start(C.karena[:, (g % 2) * 2048:((g % 2) + 1) * 2048],
                            t["kn_scr"][rows: rows + 128, :]).then_inc(s["kv"], 16)
                rows = (batch * KV + g) * RD
                e.dma_start(C.krarena[0:64, (g % 2) * 2048:((g % 2) + 1) * 2048],
                            t["kr_scr"][rows: rows + RD, :]).then_inc(s["kv"], 16)
            if h >= 2:
                e.wait_ge(s["sc"], b["sc"] + _AHEND[h - 2])
            rows = (batch * H + h) * 128
            e.dma_start(C.qn_a[:, (h % 2) * 2048:((h % 2) + 1) * 2048],
                        t["qn_scr"][rows: rows + 128, :]).then_inc(s["qh"], 16)
            rows = (batch * H + h) * RD
            e.dma_start(C.qr_a[0:64, (h % 2) * 2048:((h % 2) + 1) * 2048],
                        t["qr_scr"][rows: rows + RD, :]).then_inc(s["qh"], 16)

    if eng == "tensor":
        nhqb = 0
        pend = []
        e.wait_ge(s["lns"], b["lns"] + 2)
        for (h, qb, kt, masked, gi) in tiles:
            g = h // 4
            slot = gi % 2
            pS = C.pS[:, slot * 512:(slot + 1) * 512]
            if kt == 0:
                e.wait_ge(s["kv"], b["kv"] + 16 * (4 + 2 * (g + 1)))
                e.wait_ge(s["qh"], b["qh"] + 16 * 2 * (h + 1))
            if gi >= 2:
                e.wait_ge(s["ex"], b["ex"] + gi - 1)
            e.matmul(pS, C.karena[:, (g % 2) * 2048 + kt * 128:(g % 2) * 2048 + kt * 128 + 128],
                     C.qn_a[:, (h % 2) * 2048 + qb * 512:(h % 2) * 2048 + (qb + 1) * 512],
                     start=True, stop=False)
            e.matmul(pS, C.krarena[:, (g % 2) * 2048 + kt * 128:(g % 2) * 2048 + kt * 128 + 128],
                     C.qr_a[:, (h % 2) * 2048 + qb * 512:(h % 2) * 2048 + (qb + 1) * 512],
                     start=False, stop=True).then_inc(s["sc"], 1)
            while pend:
                kind, n = pend.pop(0)
                if kind == "tr":
                    e.wait_ge(s["zr"], b["zr"] + 4 * n + 1)
                    for qc in range(4):
                        ins = e.transpose(C.pR[0:1, qc * 128:(qc + 1) * 128],
                                          C.rzbuf[:, qc:qc + 1], C.identf)
                    ins.then_inc(s["zr"], 1)
                else:
                    e.wait_ge(s["zr"], b["zr"] + 4 * n + 3)
                    e.wait_ge(s["on"], b["on"] + n)
                    e.matmul(C.pBC, C.onesrowb, C.rowbuf[0:1, 0:512],
                             start=True, stop=True).then_inc(s["zr"], 1)
            e.wait_ge(s["ex"], b["ex"] + gi + 1)
            if masked:
                e.wait_ge(s["em"], b["em"] + _AEM[gi])
            if kt == 0:
                if nhqb >= 1:
                    e.wait_ge(s["on"], b["on"] + nhqb - 1)
                    e.wait_ge(s["zr"], b["zr"] + 4 * (nhqb - 1) + 1)
            E = C.esb[:, slot * 512:(slot + 1) * 512]
            pO = C.pO[:, (nhqb % 2) * 512:((nhqb % 2) + 1) * 512]
            last = (gi == _AHQBL[(h, qb)])
            e.matmul(pO, C.vsb[:, kt * 512 + (g % 4) * 128: kt * 512 + (g % 4) * 128 + 128],
                     E, start=(kt == 0), stop=last)
            for qc in range(4):
                ins = e.matmul(C.pZ[:, qc:qc + 1], E[:, qc * 128:(qc + 1) * 128],
                               C.onescol, start=(kt == 0 and qc == 0),
                               stop=(last and qc == 3))
            ins.then_inc(s["av"], 1)
            if last:
                pend.append(("tr", nhqb))
                pend.append(("ou", nhqb))
                nhqb += 1
        for kind, n in pend:
            if kind == "tr":
                e.wait_ge(s["zr"], b["zr"] + 4 * n + 1)
                for qc in range(4):
                    ins = e.transpose(C.pR[0:1, qc * 128:(qc + 1) * 128],
                                      C.rzbuf[:, qc:qc + 1], C.identf)
                ins.then_inc(s["zr"], 1)
            else:
                e.wait_ge(s["zr"], b["zr"] + 4 * n + 3)
                e.wait_ge(s["on"], b["on"] + n)
                e.matmul(C.pBC, C.onesrowb, C.rowbuf[0:1, 0:512],
                         start=True, stop=True).then_inc(s["zr"], 1)

    if eng == "scalar":
        for (h, qb, kt, masked, gi) in tiles:
            e.wait_ge(s["sc"], b["sc"] + gi + 1)
            if gi >= 2:
                e.wait_ge(s["av"], b["av"] + gi - 1)
            slot = gi % 2
            e.activation(C.esb[:, slot * 512:(slot + 1) * 512],
                         C.pS[:, slot * 512:(slot + 1) * 512], AF.Exp,
                         bias=C.negB[:, batch * H + h: batch * H + h + 1],
                         scale=SCALE).then_inc(s["ex"], 1)

    if eng == "vector":
        e.memset(C.krarena[64:128, 0:2048], 0.0)
        e.memset(C.krarena[64:128, 2048:4096], 0.0).then_inc(s["lns"], 2)
        nhqb = 0
        for (h, qb, kt, masked, gi) in tiles:
            slot = gi % 2
            if masked:
                e.wait_ge(s["ex"], b["ex"] + gi + 1)
                dtile = kt - 4 * qb
                e.tensor_mul(C.esb[:, slot * 512:(slot + 1) * 512],
                             C.esb[:, slot * 512:(slot + 1) * 512],
                             C.bmask[:, dtile * 512:(dtile + 1) * 512]).then_inc(s["em"], 1)
            if gi == _AHQBL[(h, qb)]:
                e.wait_ge(s["av"], b["av"] + gi + 1)
                e.reciprocal(C.rzbuf[:, 0:4], C.pZ[:, 0:4]).then_inc(s["zr"], 1)
                e.wait_ge(s["zr"], b["zr"] + 4 * nhqb + 2)
                e.tensor_copy(C.rowbuf[0:1, 0:512], C.pR[0:1, 0:512]).then_inc(s["zr"], 1)
                e.wait_ge(s["zr"], b["zr"] + 4 * nhqb + 4)
                e.tensor_copy(C.rbsb, C.pBC)
                e.tensor_tensor(C.obuf[:, h * 2048 + qb * 512: h * 2048 + (qb + 1) * 512],
                                C.pO[:, (nhqb % 2) * 512:((nhqb % 2) + 1) * 512],
                                C.rbsb, ALU.mult).then_inc(s["on"], 1)
                nhqb += 1

    for k, d in (("kv", 16 * 12), ("qh", 16 * 32), ("sc", NTILE), ("ex", NTILE),
                 ("em", NMASK), ("av", NTILE), ("zr", 4 * 64), ("on", 64), ("lns", 2)):
        b[k] += d
    return b



def _emit_attb(eng, e, C, b, batch):
    s = C.s
    t = C.t
    _gate(e, C, b, GATE_ATTB)

    if eng == "gpsimd":
        for hh in range(H):
            e.dma_start(C.wosb[:, hh * 2048:(hh + 1) * 2048],
                        t["won"][hh * 128:(hh + 1) * 128, :]).then_inc(s["wol"], 16)

    if eng == "sync":
        for w in range(8):
            for tt in range(2):
                tokb = w * 2 + tt
                if tokb >= 2:
                    e.wait_ge(s["yq"], b["yq"] + tokb - 1)
                rows = (batch * 16 + tokb) * 128
                e.dma_start(C.yfslot[:, (tokb % 2) * 2048:((tokb % 2) + 1) * 2048],
                            t["yf_scr"][rows: rows + 128, :]).then_inc(s["yw"], 16)
                e.dma_start(C.xtile[:, (tokb % 2) * 2048:((tokb % 2) + 1) * 2048],
                            t["x"][batch * S + tokb * 128: batch * S + (tokb + 1) * 128, :]
                            ).then_inc(s["yw"], 16)
            for tt in range(2):
                tokb = w * 2 + tt
                e.wait_ge(s["yq"], b["yq"] + tokb + 1)
                e.dma_start(t["out"][batch * S + tokb * 128: batch * S + (tokb + 1) * 128, :],
                            C.ostage[:, tt * 2048:(tt + 1) * 2048]).then_inc(s["outd"], 16)

    if eng == "tensor":
        e.wait_ge(s["wol"], b["wol"] + 16 * H)
        for w in range(8):
            if w >= 1:
                e.wait_ge(s["yq"], b["yq"] + 2 * w)
            for hh in range(H):
                for tt in range(2):
                    pt = C.pA if tt == 0 else C.pB
                    for hg in range(4):
                        ins = e.matmul(
                            pt[:, hg * 512:(hg + 1) * 512],
                            C.obuf[:, hh * 2048 + w * 256 + tt * 128: hh * 2048 + w * 256 + tt * 128 + 128],
                            C.wosb[:, hh * 2048 + hg * 512: hh * 2048 + (hg + 1) * 512],
                            start=(hh == 0), stop=(hh == H - 1))
            ins.then_inc(s["wo"], 1)

    if eng == "vector":
        for w in range(8):
            e.wait_ge(s["wo"], b["wo"] + w + 1)
            for tt in range(2):
                tokb = w * 2 + tt
                pt = C.pA if tt == 0 else C.pB
                e.wait_ge(s["yw"], b["yw"] + 32 * (tokb + 1))
                if w >= 1:
                    e.wait_ge(s["outd"], b["outd"] + 16 * (2 * (w - 1) + tt + 1))
                e.tensor_add(C.ostage[:, tt * 2048:(tt + 1) * 2048], pt[:, 0:2048],
                             C.yfslot[:, (tokb % 2) * 2048:((tokb % 2) + 1) * 2048])
                e.tensor_add(C.ostage[:, tt * 2048:(tt + 1) * 2048],
                             C.ostage[:, tt * 2048:(tt + 1) * 2048],
                             C.xtile[:, (tokb % 2) * 2048:((tokb % 2) + 1) * 2048]
                             ).then_inc(s["yq"], 1)

    for k, d in (("wol", 16 * H), ("yw", 16 * 32), ("wo", 8), ("yq", 16),
                 ("outd", 16 * 16)):
        b[k] += d
    return b



def _build_program():
    from contextlib import ExitStack
    nc = bass.Bass("TRN2", target_bir_lowering=False)
    stack = ExitStack()
    with stack:
        C = _declare(nc, stack)
        _alloc_overlays(nc, C)
        with nc.Block() as block:
            def body(eng):
                def f(e):
                    s = C.s
                    if eng == "gpsimd":
                        for ap, src in ((C.identb, C.t["identb"]), (C.identf, C.t["identf"]),
                                        (C.onescol, C.t["onescol"]), (C.onesrow, C.t["onesrow"]),
                                        (C.onesrowb, C.t["onesrowb"]),
                                        (C.cosb, C.t["cosb"]), (C.sinb, C.t["sinb"]),
                                        (C.bmask, C.t["bmask"])):
                            e.dma_start(ap, src).then_inc(s["cv"], 16)
                        e.memset(C.epsap, EPS).then_inc(s["cv"], 1)
                    b = _zero_counts()
                    b["cv"] = 8 * 16 + 1
                    if eng in ("tensor", "vector", "scalar"):
                        e.wait_ge(s["cv"], 8 * 16 + 1)
                    for batch in range(BPC):
                        if "p1" in PHASES:
                            b = _emit_p1(eng, e, C, b, batch)
                        if "p2" in PHASES:
                            b = _emit_p2(eng, e, C, b, batch)
                        if "ffn" in PHASES:
                            b = _emit_ffn(eng, e, C, b, batch)
                        if "att" in PHASES:
                            b = _emit_atta(eng, e, C, b, batch)
                            b = _emit_attb(eng, e, C, b, batch)
                    if eng == "gpsimd":
                        for n in ("xnw", "wrt", "cvw", "yo", "outd"):
                            if b[n] > 0:
                                e.wait_ge(s[n], b[n])
                return f
            block.gpsimd(body("gpsimd"))
            block.tensor(body("tensor"))
            block.vector(body("vector"))
            block.scalar(body("scalar"))
            block.sync(body("sync"))
    return nc


def _get_nc():
    if "nc" not in _CACHE:
        _CACHE["nc"] = _build_program()
    return _CACHE["nc"]


def kernel(**inputs):
    in_maps = _host_prep(inputs)
    nc = _get_nc()
    res = run_bass_kernel_spmd(nc, in_maps, core_ids=list(range(NCORES)))
    out = np.empty((B_ALL, S, D), np.float32)
    for cid in range(NCORES):
        out[BPC * cid: BPC * (cid + 1)] = res.results[cid]["out"].reshape(BPC, S, D)
    if DEBUG_DUMPS:
        kernel.last_results = res.results
    return out


# version 03
# speedup vs baseline: 1.1670x; 1.1670x over previous
import math
import numpy as np
import ml_dtypes
import concourse.bass as bass
import concourse.mybir as mybir
from concourse.bass_utils import run_bass_kernel_spmd

F32 = mybir.dt.float32
BF16 = mybir.dt.bfloat16
AF = mybir.ActivationFunctionType
ALU = mybir.AluOpType
AXX = mybir.AxisListType.X
BF = ml_dtypes.bfloat16

B_ALL, S, D = 16, 2048, 2048
NCORES = 8
BPC = B_ALL // NCORES
TOKS = BPC * S
NH = D // 128
H, KV, HD, RD = 16, 4, 128, 64
NR2 = RD // 2
IFF = 8192
NQUAD = 16
WD_GRP = 4
NGRP = 64 // WD_GRP
SCALE = 1.0 / math.sqrt(HD + RD)
NBLK = S // 512
EPS = 1e-5

DEBUG_DUMPS = ()
PHASES = ("p1", "p2", "ffn", "att")

_CACHE = {}



def _ffn_perm():
    q, m, p = np.meshgrid(np.arange(16), np.arange(4), np.arange(128), indexing="ij")
    perm = np.empty(IFF, np.int64)
    perm[(q * 512 + m * 128 + p).ravel()] = (4 * (q * 128 + p) + m).ravel()
    return perm


def _rope_perm(nheads):
    cols = []
    for x2 in (0, 1):
        for h in range(nheads):
            for f in range(NR2):
                cols.append(h * RD + 2 * f + x2)
    return np.asarray(cols, np.int64)


def _make_wg_scr(w):
    return np.ascontiguousarray(w.reshape(16, 128, 16, 512).transpose(2, 1, 0, 3).reshape(2048, 8192))


def _make_wd_scr(w):
    return np.ascontiguousarray(w.reshape(64, 128, 2, 1024).transpose(2, 0, 1, 3).reshape(16384, 1024))


def _host_consts():
    c = {}
    c["identb"] = np.eye(128, dtype=BF)
    c["identf"] = np.eye(128, dtype=np.float32)
    c["onescol"] = np.ones((128, 1), dtype=BF)
    c["onesrow"] = np.ones((1, 128), dtype=np.float32)
    c["onesrowb"] = np.ones((1, 128), dtype=BF)
    freqs = 1.0 / (10000.0 ** (np.arange(0, RD, 2, dtype=np.float64) / RD))
    t = np.arange(S, dtype=np.float64)
    ang = np.outer(freqs, t)
    c["cosb"] = np.tile(np.cos(ang), (4, 1)).astype(BF)
    c["sinb"] = np.tile(np.sin(ang), (4, 1)).astype(BF)
    kk = np.arange(128)[:, None]
    qq = np.arange(512)[None, :]
    c["bmask"] = np.ascontiguousarray(
        np.concatenate([(qq >= kk + d).astype(BF) for d in (0, 128, 256, 384)], axis=1))
    return c


def _host_prep(inputs):
    x = np.asarray(inputs["L0_attn_hidden_states"])
    fperm = _ffn_perm()
    Wo = np.asarray(inputs["L0_attn_Wo"])
    shared = {
        "wq": np.ascontiguousarray(np.asarray(inputs["L0_attn_Wq"])),
        "wqr": np.ascontiguousarray(np.asarray(inputs["L0_attn_Wqrope"])[:, _rope_perm(H)]),
        "wkvc": np.ascontiguousarray(np.asarray(inputs["L0_attn_Wkvc"])),
        "wkdec": np.ascontiguousarray(np.asarray(inputs["L0_attn_Wkdec"])),
        "wvdec": np.ascontiguousarray(np.asarray(inputs["L0_attn_Wvdec"])),
        "wkr": np.ascontiguousarray(np.asarray(inputs["L0_attn_Wkrope"])[:, _rope_perm(KV)]),
        "won": np.ascontiguousarray(Wo.reshape(H, HD + RD, D)[:, :HD].reshape(H * HD, D)),
        "wg32": _make_wg_scr(np.ascontiguousarray(np.asarray(inputs["L0_ffn_w_gate"])[:, fperm])),
        "wu32": _make_wg_scr(np.ascontiguousarray(np.asarray(inputs["L0_ffn_w_up"])[:, fperm])),
        "wd32": _make_wd_scr(np.ascontiguousarray(np.asarray(inputs["L0_ffn_w_down"])[fperm, :])),
    }
    shared.update(_host_consts())
    in_maps = []
    for cid in range(NCORES):
        m = dict(shared)
        m["x"] = np.ascontiguousarray(x[BPC * cid: BPC * (cid + 1)].reshape(TOKS, D))
        in_maps.append(m)
    return in_maps



class _Ctx:
    pass


_TENSORS = [
    ("x", [TOKS, D], F32, "ExternalInput"),
    ("wq", [D, D], F32, "ExternalInput"),
    ("wqr", [D, H * RD], F32, "ExternalInput"),
    ("wkvc", [D, 512], F32, "ExternalInput"),
    ("wkdec", [512, 512], F32, "ExternalInput"),
    ("wvdec", [512, 512], F32, "ExternalInput"),
    ("wkr", [512, KV * RD], F32, "ExternalInput"),
    ("won", [H * HD, D], F32, "ExternalInput"),
    ("wg32", [2048, 8192], F32, "ExternalInput"),
    ("wu32", [2048, 8192], F32, "ExternalInput"),
    ("wd32", [16384, 1024], F32, "ExternalInput"),
    ("identb", [128, 128], BF16, "ExternalInput"),
    ("identf", [128, 128], F32, "ExternalInput"),
    ("onescol", [128, 1], BF16, "ExternalInput"),
    ("onesrow", [1, 128], F32, "ExternalInput"),
    ("onesrowb", [1, 128], BF16, "ExternalInput"),
    ("cosb", [128, S], BF16, "ExternalInput"),
    ("sinb", [128, S], BF16, "ExternalInput"),
    ("bmask", [128, 4 * 512], BF16, "ExternalInput"),
    ("out", [TOKS, D], F32, "ExternalOutput"),
    ("xnT_scr", [BPC * NBLK * 128, NH * 512], BF16, "Internal"),
    ("qn_scr", [BPC * H * 128, S], BF16, "Internal"),
    ("qr_scr", [BPC * H * RD, S], BF16, "Internal"),
    ("kn_scr", [BPC * KV * 128, S], BF16, "Internal"),
    ("kr_scr", [BPC * KV * RD, S], BF16, "Internal"),
    ("v_scr", [BPC * 2 * 128, 4096], BF16, "Internal"),
    ("yf_scr", [TOKS, D], BF16, "Internal"),
    ("wg_scr", [2048, 8192], BF16, "Internal"),
    ("wu_scr", [2048, 8192], BF16, "Internal"),
    ("wd_scr", [16384, 1024], BF16, "Internal"),
]

SEM_NAMES = [
    "x", "xf", "sqa", "ln", "sq2", "ln2", "tp", "tpf", "xnw", "lns",
    "cv", "cvw",
    "aw", "mm", "ev", "sq", "nm", "nr", "rp", "wrt",
    "w", "wf", "g", "sil", "ubc", "hm", "hq", "wd", "wdf", "ye",
    "yo", "xs",
    "kv", "qh", "sc", "ex", "em", "zr", "on", "wo", "yq", "outd", "wol",
    "av", "yw",
]


def _zero_counts():
    return {n: 0 for n in SEM_NAMES}


def _declare(nc, stack):
    C = _Ctx()
    C.t = {}
    for name, shp, dt, kind in _TENSORS:
        if name in DEBUG_DUMPS and kind == "Internal":
            kind = "ExternalOutput"
        C.t[name] = nc.dram_tensor(name, shp, dt, kind=kind).ap()
    C.s = {n: stack.enter_context(nc.semaphore("s_" + n)) for n in SEM_NAMES}

    sb = lambda name, shp, dt: stack.enter_context(nc.sbuf_tensor(name, shp, dt))
    C.identb = sb("identb_sb", [128, 128], BF16).ap()
    C.identf = sb("identf_sb", [128, 128], F32).ap()
    C.onescol = sb("onescol_sb", [128, 1], BF16).ap()
    C.onesrow = sb("onesrow_sb", [1, 128], F32).ap()
    C.cosb = sb("cosb_sb", [128, S], BF16).ap()
    C.sinb = sb("sinb_sb", [128, S], BF16).ap()
    C.bmask = sb("bmask_sb", [128, 4 * 512], BF16).ap()
    C.negB = sb("negB_sb", [128, 2 * H], F32).ap()
    C.mx = sb("mx_sb", [1, 64], F32).ap()
    C.small = sb("small_sb", [128, 32], F32).ap()
    C.rowbuf = sb("rowbuf", [1, 512], F32).ap()
    C.rowbufb = sb("rowbufb", [1, 512], BF16).ap()
    C.onesrowb = sb("onesrowb_sb", [1, 128], BF16).ap()
    C.epsap = sb("eps_sb", [128, 1], F32).ap()

    C.blkA = sb("blkA", [128, 32768], BF16).ap()
    C.blkB = sb("blkB", [128, 32768], BF16).ap()
    C.stq = sb("stq", [128, 8192], BF16).ap()
    C.xtile = sb("xtile", [128, 4096], F32).ap()
    C.pool = sb("pool", [128, 15360], BF16).ap()
    return C


def _alloc_overlays(nc, C):
    aA = nc.lookup_mloc(C.blkA.tensor).addr
    aB = nc.lookup_mloc(C.blkB.tensor).addr
    aQ = nc.lookup_mloc(C.stq.tensor).addr
    aX = nc.lookup_mloc(C.xtile.tensor).addr
    aP = nc.lookup_mloc(C.pool.tensor).addr
    at = lambda name, shp, dt, off: nc.alloc_sbuf_tensor_at(name, shp, dt, offset=off).ap()

    C.xnT = at("xnT", [128, NH * S], BF16, aA)
    C.warena = at("warena", [128, 32768], BF16, aB)
    C.stage = at("stage", [128, 2 * 2048], BF16, aQ)
    C.sqs = at("sqs", [128, 2 * 2048], BF16, aQ + 8192)

    C.convb = at("convb", [128, 2 * 4096], BF16, aP)

    C.ctb = at("ctb", [128, 4 * 2048], BF16, aX)
    C.tmp0 = at("tmp0", [128, 2048], BF16, aP + 16384)
    C.tmp1 = at("tmp1", [128, 2048], BF16, aP + 20480)

    C.h_sp = at("h_sp", [128, 64 * 512], BF16, aA)
    C.wgwu = at("wgwu", [128, 2 * 2 * NH * 512], BF16, aB)
    C.xnslot = at("xnslot", [128, NH * 512], BF16, aQ)
    C.wdslot = at("wdslot", [128, 2 * WD_GRP * 1024], BF16, aX)
    C.sg = at("sg", [128, 4 * 512], BF16, aP)
    C.ub = at("ub", [128, 4 * 512], BF16, aP + 4096)
    C.sq4 = at("sq4", [128, 4 * 512], BF16, aP + 8192)
    C.cmp2 = at("cmp2", [128, 2 * 512], BF16, aP + 12288)
    C.ystage = at("ystage", [128, 4 * 2048], BF16, aP + 14336)

    C.obuf = at("obuf", [128, H * S], BF16, aB)
    off = aA
    def ata(name, shp, dt):
        nonlocal off
        ap = at(name, shp, dt, off)
        nbytes = int(np.prod(shp[1:])) * (2 if dt == BF16 else 4)
        off += (nbytes + 31) & ~31
        return ap
    C.vsb = ata("vsb", [128, 16 * 512], BF16)
    C.karena = ata("karena", [128, 2 * 2048], BF16)
    C.krarena = ata("krarena", [128, 2 * 2048], BF16)
    C.qn_a = ata("qn_a", [128, 2 * 2048], BF16)
    C.qr_a = ata("qr_a", [128, 2 * 2048], BF16)
    C.esb = ata("esb", [128, 2 * 512], BF16)
    C.rzbuf = ata("rzbuf", [128, 8], F32)
    C.rbsb = ata("rbsb", [128, 512], F32)
    assert off - aA <= 53 * 1024

    C.wosb = at("wosb", [128, H * 2048], BF16, aA)
    C.yfslot = at("yfslot", [128, 2 * 2048], BF16, aP)
    C.ostage = at("ostage", [128, 2 * 2048], F32, aP + 8192)

    C.pA = nc.alloc_psum_tensor("pA", [128, 2048], F32).ap()
    C.pB = nc.alloc_psum_tensor("pB", [128, 2048], F32).ap()
    saved = nc.psum_base
    nc.psum_base = 0
    C.ptp = nc.alloc_psum_tensor("ptp", [128, 2 * 2048], BF16).ap()
    nc.psum_base = 0
    C.pS = nc.alloc_psum_tensor("pS", [128, 2 * 512], F32).ap()
    C.pO = nc.alloc_psum_tensor("pO", [128, 2 * 512], F32).ap()
    C.pZ = nc.alloc_psum_tensor("pZ", [1, 512], F32).ap()
    C.pBC = nc.alloc_psum_tensor("pBC", [128, 512], F32).ap()
    nc.psum_base = saved
    return C


def _gate(e, C, b, names):
    for n in names:
        if b[n] > 0:
            e.wait_ge(C.s[n], b[n])


GATE_P1 = ("wo", "yq", "outd", "wol", "wdf", "hq", "ye", "yo", "xs", "wrt", "nr")
GATE_P2 = ("tp", "tpf", "sq2", "xnw")
GATE_FFN = ("mm", "ev", "sq", "nr", "wrt", "aw", "cvw")
GATE_ATTA = ("wdf", "hq", "ye", "w", "wd", "yo", "xs")
GATE_ATTB = ("on", "ex", "em", "kv", "qh", "zr")



def _emit_p1(eng, e, C, b, batch):
    s = C.s
    t = C.t
    nsub = 16
    _gate(e, C, b, GATE_P1)
    if eng == "sync":
        for i in range(nsub):
            slot = i % 2
            prev = b["ln2"] + i - 2
            if prev >= 0:
                e.wait_ge(s["ln2"], prev + 1)
            e.dma_start(C.xtile[:, slot * 2048:(slot + 1) * 2048],
                        t["x"][batch * S + i * 128: batch * S + (i + 1) * 128, :]
                        ).then_inc(s["x"], 16)
        for blk in range(NBLK):
            e.wait_ge(s["tpf"], b["tpf"] + (blk + 1) * 4)
            rows = (batch * NBLK + blk) * 128
            e.dma_start(t["xnT_scr"][rows: rows + 128, :],
                        C.xnT[:, blk * 8192:(blk + 1) * 8192]).then_inc(s["xnw"], 16)

    if eng == "scalar":
        for i in range(nsub):
            slot = i % 2
            sm = C.small[:, slot * 8: slot * 8 + 8]
            e.wait_ge(s["ln"], b["ln"] + i + 1)
            e.activation(sm[:, 4:5], sm[:, 3:4], AF.Sqrt, bias=C.epsap).then_inc(s["sq2"], 1)

    if eng == "vector":
        if b["xnw"] > 0:
            e.wait_ge(s["xnw"], b["xnw"])

        def evac(j):
            e.wait_ge(s["tp"], b["tp"] + j + 1)
            dst = C.xnT[:, (j // 4) * 8192:(j // 4 + 1) * 8192]
            e.tensor_copy(
                dst.rearrange("p (h t) -> p h t", h=NH)[:, :, (j % 4) * 128:(j % 4 + 1) * 128],
                C.ptp[:, (j % 2) * 2048:((j % 2) + 1) * 2048].rearrange("p (h t) -> p h t", h=NH)
            ).then_inc(s["tpf"], 1)

        for i in range(nsub):
            slot = i % 2
            xap = C.xtile[:, slot * 2048:(slot + 1) * 2048]
            sm = C.small[:, slot * 8: slot * 8 + 8]
            e.wait_ge(s["x"], b["x"] + (i + 1) * 16)
            e.reduce_sum(sm[:, 1:2], xap, AXX).then_inc(s["lns"], 1)
            e.tensor_mul(C.sqs[:, slot * 2048:(slot + 1) * 2048], xap, xap)
            e.reduce_sum(sm[:, 0:1], C.sqs[:, slot * 2048:(slot + 1) * 2048], AXX
                         ).then_inc(s["lns"], 1)
            e.wait_ge(s["lns"], b["lns"] + 5 * i + 2)
            e.tensor_scalar(sm[:, 2:3], sm[:, 1:2], 1.0 / D, None, ALU.mult
                            ).then_inc(s["lns"], 1)
            e.wait_ge(s["lns"], b["lns"] + 5 * i + 3)
            e.tensor_mul(sm[:, 5:6], sm[:, 2:3], sm[:, 2:3]).then_inc(s["lns"], 1)
            e.wait_ge(s["lns"], b["lns"] + 5 * i + 4)
            e.tensor_scalar(sm[:, 3:4], sm[:, 0:1], 1.0 / D, sm[:, 5:6],
                            ALU.mult, ALU.subtract).then_inc(s["ln"], 1)
            e.wait_ge(s["sq2"], b["sq2"] + i + 1)
            e.reciprocal(sm[:, 6:7], sm[:, 4:5]).then_inc(s["lns"], 1)
            if b["tp"] + i - 1 >= 1:
                e.wait_ge(s["tp"], b["tp"] + i - 1)
            e.wait_ge(s["lns"], b["lns"] + 5 * i + 5)
            e.tensor_scalar(C.stage[:, slot * 2048:(slot + 1) * 2048], xap,
                            sm[:, 2:3], sm[:, 6:7], ALU.subtract, ALU.mult
                            ).then_inc(s["ln2"], 1)
            if i >= 1:
                evac(i - 1)
        evac(nsub - 1)

    if eng == "tensor":
        for i in range(nsub):
            slot = i % 2
            e.wait_ge(s["ln2"], b["ln2"] + i + 1)
            if b["tpf"] + i - 1 >= 1:
                e.wait_ge(s["tpf"], b["tpf"] + i - 1)
            for hc in range(NH):
                ins = e.transpose(C.ptp[:, slot * 2048 + hc * 128: slot * 2048 + (hc + 1) * 128],
                                  C.stage[:, slot * 2048 + hc * 128: slot * 2048 + (hc + 1) * 128],
                                  C.identb)
            ins.then_inc(s["tp"], 1)

    for k, d in (("x", 256), ("ln", 16), ("sq2", 16), ("lns", 5 * 16),
                 ("ln2", 16), ("tp", 16), ("tpf", 16), ("xnw", 16 * NBLK)):
        b[k] += d
    return b



def _p2_chunks():
    out = []
    for pr in range(4):
        out.append(("qr1", pr))
        out.append(("qr2", pr))
    for c in range(4):
        out.append(("c", c))
    for g in range(KV):
        out.append(("kn", g))
    out.append(("kr1", 0))
    out.append(("kr2", 0))
    for kt in range(16):
        out.append(("v", kt))
    for h in range(H):
        out.append(("qn", h))
    return out


_PAIR1 = ("qr1", "kr1")
_PAIR2 = ("qr2", "kr2")
_NORMED = {"qn": True, "qr1": True, "qr2": True, "c": False, "kn": True,
           "kr1": True, "kr2": True, "v": False}


def _p2_sched():
    chunks = _p2_chunks()
    nC = len(chunks)
    mm_at, ev_at, sq_at, nm_at, nr_at, wrt_at = {}, {}, {}, {}, {}, {}
    psum_name, stage_slot, nst_at = {}, {}, {}
    cmm = cev = csq = cnm = cwrt = 0
    nbig = 0
    nst = 0
    wrt_of = {"qn": 1, "qr1": 0, "qr2": 8, "c": 0, "kn": 1, "kr1": 0, "kr2": 8, "v": 0}
    for i, (kind, idx) in enumerate(chunks):
        cmm += 1
        mm_at[i] = cmm
        cev += 1
        ev_at[i] = cev
        if _NORMED[kind]:
            csq += 1
            cnm += 1
        sq_at[i] = csq
        nm_at[i] = cnm
        nr_at[i] = cnm
        cwrt += wrt_of[kind]
        if kind == "v" and idx in (7, 15):
            cwrt += 1
        wrt_at[i] = cwrt
        if kind in _PAIR1:
            psum_name[i] = "A"
        elif kind in _PAIR2:
            psum_name[i] = "B"
        elif kind == "v":
            psum_name[i] = "A" if (idx // 4) % 2 == 0 else "B"
        else:
            psum_name[i] = "A" if nbig % 2 == 0 else "B"
            nbig += 1
        if kind in ("qn", "kn"):
            stage_slot[i] = nst % 2
            nst_at[i] = nst
            nst += 1
    tot = dict(mm=cmm, ev=cev, sq=csq, nm=cnm, nr=cnm, wrt=cwrt)
    return chunks, dict(mm_at=mm_at, ev_at=ev_at, sq_at=sq_at, nm_at=nm_at,
                        nr_at=nr_at, wrt_at=wrt_at, psum_name=psum_name,
                        stage_slot=stage_slot, nst_at=nst_at, tot=tot)


_P2C, _P2S = _p2_sched()


def _emit_p2(eng, e, C, b, batch):
    s = C.s
    t = C.t
    _gate(e, C, b, GATE_P2)
    chunks, sch = _P2C, _P2S
    nC = len(chunks)
    tot = sch["tot"]
    mm_at, ev_at, sq_at = sch["mm_at"], sch["ev_at"], sch["sq_at"]
    nm_at, nr_at, wrt_at = sch["nm_at"], sch["nr_at"], sch["wrt_at"]

    def psum_of(i):
        base = C.pA if sch["psum_name"][i] == "A" else C.pB
        kind, idx = chunks[i]
        if kind == "v":
            return base[:, (idx % 4) * 512:((idx % 4) + 1) * 512]
        return base

    def prev_user(i):
        for j in range(i - 1, -1, -1):
            if sch["psum_name"][j] == sch["psum_name"][i]:
                return j
        return None

    def stage_of(i):
        kind = chunks[i][0]
        if kind in _PAIR1:
            return C.stage[:, 0:2048]
        if kind in _PAIR2:
            return C.stage[:, 2048:4096]
        sl = sch["stage_slot"][i]
        return C.stage[:, sl * 2048:(sl + 1) * 2048]

    def vstage(kt):
        off = (kt % 8) * 512
        return C.stage[:, off: off + 512]

    def sq_slot(i):
        nsq = sq_at[i]
        return C.sqs[:, ((nsq - 1) % 2) * 2048:(((nsq - 1) % 2) + 1) * 2048]

    def stage_prev(i):
        nst = sch["nst_at"][i]
        if nst < 2:
            return None
        for j in range(i):
            if j in sch["nst_at"] and sch["nst_at"][j] == nst - 2:
                return j
        return None

    def main_mms(kind, idx, X):
        ins = None
        if kind == "qn":
            for hc in range(NH):
                for tb in range(4):
                    ins = e.matmul(X[:, tb * 512:(tb + 1) * 512],
                                   C.warena[:, hc * 2048 + idx * 128: hc * 2048 + idx * 128 + 128],
                                   C.xnT[:, tb * 8192 + hc * 512: tb * 8192 + hc * 512 + 512],
                                   start=(hc == 0), stop=(hc == NH - 1))
        elif kind in ("qr1", "qr2"):
            cc = idx if kind == "qr1" else 4 + idx
            for hc in range(NH):
                for tb in range(4):
                    ins = e.matmul(X[:, tb * 512:(tb + 1) * 512],
                                   C.warena[:, hc * 1024 + cc * 128: hc * 1024 + cc * 128 + 128],
                                   C.xnT[:, tb * 8192 + hc * 512: tb * 8192 + hc * 512 + 512],
                                   start=(hc == 0), stop=(hc == NH - 1))
        elif kind == "c":
            for hc in range(NH):
                for tb in range(4):
                    ins = e.matmul(X[:, tb * 512:(tb + 1) * 512],
                                   C.warena[:, 16384 + hc * 512 + idx * 128: 16384 + hc * 512 + idx * 128 + 128],
                                   C.xnT[:, tb * 8192 + hc * 512: tb * 8192 + hc * 512 + 512],
                                   start=(hc == 0), stop=(hc == NH - 1))
        elif kind == "kn":
            for cc in range(4):
                for tb in range(4):
                    ins = e.matmul(X[:, tb * 512:(tb + 1) * 512],
                                   C.warena[:, 24576 + cc * 512 + idx * 128: 24576 + cc * 512 + idx * 128 + 128],
                                   C.ctb[:, cc * 2048 + tb * 512: cc * 2048 + (tb + 1) * 512],
                                   start=(cc == 0), stop=(cc == 3))
        elif kind in ("kr1", "kr2"):
            x0 = 0 if kind == "kr1" else 128
            for cc in range(4):
                for tb in range(4):
                    ins = e.matmul(X[:, tb * 512:(tb + 1) * 512],
                                   C.warena[:, 28672 + cc * 256 + x0: 28672 + cc * 256 + x0 + 128],
                                   C.ctb[:, cc * 2048 + tb * 512: cc * 2048 + (tb + 1) * 512],
                                   start=(cc == 0), stop=(cc == 3))
        elif kind == "v":
            for cc in range(4):
                ins = e.matmul(X,
                               C.ctb[:, cc * 2048 + idx * 128: cc * 2048 + idx * 128 + 128],
                               C.warena[:, 26624 + cc * 512: 26624 + (cc + 1) * 512],
                               start=(cc == 0), stop=(cc == 3))
        return ins

    if eng == "gpsimd":
        for hc in range(NH):
            e.dma_start(C.warena[:, hc * 1024:(hc + 1) * 1024],
                        t["wqr"][hc * 128:(hc + 1) * 128, :]).then_inc(s["aw"], 16)
        for hc in range(NH):
            e.dma_start(C.warena[:, 16384 + hc * 512: 16384 + (hc + 1) * 512],
                        t["wkvc"][hc * 128:(hc + 1) * 128, :]).then_inc(s["aw"], 16)
        for cc in range(4):
            e.dma_start(C.warena[:, 24576 + cc * 512: 24576 + (cc + 1) * 512],
                        t["wkdec"][cc * 128:(cc + 1) * 128, :]).then_inc(s["aw"], 16)
        for cc in range(4):
            e.dma_start(C.warena[:, 26624 + cc * 512: 26624 + (cc + 1) * 512],
                        t["wvdec"][cc * 128:(cc + 1) * 128, :]).then_inc(s["aw"], 16)
        for cc in range(4):
            e.dma_start(C.warena[:, 28672 + cc * 256: 28672 + (cc + 1) * 256],
                        t["wkr"][cc * 128:(cc + 1) * 128, :]).then_inc(s["aw"], 16)
        e.wait_ge(s["mm"], b["mm"] + 8)
        for hc in range(8):
            e.dma_start(C.warena[:, hc * 2048:(hc + 1) * 2048],
                        t["wq"][hc * 128:(hc + 1) * 128, :]).then_inc(s["aw"], 16)
        e.wait_ge(s["mm"], b["mm"] + 34)
        for hc in range(8, NH):
            e.dma_start(C.warena[:, hc * 2048:(hc + 1) * 2048],
                        t["wq"][hc * 128:(hc + 1) * 128, :]).then_inc(s["aw"], 16)
        if batch == 0:
            _emit_conversion(e, C, b)

    if eng == "tensor":
        AW_SMALL = b["aw"] + 16 * (16 + 16 + 4 + 4 + 4)
        AW_ALL = b["aw"] + 16 * (16 + 16 + 16 + 4 + 4 + 4)
        done_norm = set()

        def norms(j):
            kind = chunks[j][0]
            X = C.pA if sch["psum_name"][j] == "A" else C.pB
            e.wait_ge(s["sq"], b["sq"] + sq_at[j])
            e.wait_ge(s["ev"], b["ev"] + ev_at[j])
            sqap = sq_slot(j)
            for tb in range(4):
                ins = e.matmul(X[0:1, tb * 512:(tb + 1) * 512], C.onescol,
                               sqap[:, tb * 512:(tb + 1) * 512], start=True, stop=True)
            ins.then_inc(s["nm"], 1)
            done_norm.add(j)

        for i, (kind, idx) in enumerate(chunks):
            e.wait_ge(s["aw"], AW_ALL if kind == "qn" else AW_SMALL)
            pu = prev_user(i)
            if pu is not None:
                e.wait_ge(s["ev"], b["ev"] + ev_at[pu])
                if _NORMED[chunks[pu][0]]:
                    e.wait_ge(s["nr"], b["nr"] + nr_at[pu])
            ins = main_mms(kind, idx, psum_of(i))
            ins.then_inc(s["mm"], 1)
            for j in range(i):
                if _NORMED[chunks[j][0]] and j not in done_norm:
                    norms(j)
        for j in range(nC):
            if _NORMED[chunks[j][0]] and j not in done_norm:
                norms(j)
        _p2_negb(eng, e, C, b, batch, tot)

    if eng == "vector":
        npair2 = 0
        for i, (kind, idx) in enumerate(chunks):
            X = psum_of(i)
            e.wait_ge(s["mm"], b["mm"] + mm_at[i])
            if kind == "v":
                if idx >= 8:
                    e.wait_ge(s["wrt"], b["wrt"] + 16 * wrt_at[i - 1])
                elif i >= 2:
                    e.wait_ge(s["wrt"], b["wrt"] + 16 * wrt_at[i - 2])
                    e.wait_ge(s["sq"], b["sq"] + sq_at[i - 2])
                e.tensor_copy(vstage(idx), X).then_inc(s["ev"], 1)
            elif kind == "c":
                e.tensor_copy(C.ctb[:, idx * 2048:(idx + 1) * 2048], X).then_inc(s["ev"], 1)
            elif kind in ("qn", "kn"):
                if i >= 2:
                    e.wait_ge(s["wrt"], b["wrt"] + 16 * wrt_at[i - 2])
                    e.wait_ge(s["sq"], b["sq"] + sq_at[i - 2])
                e.tensor_copy(stage_of(i), X).then_inc(s["ev"], 1)
            elif kind in _PAIR1:
                if npair2 >= 1:
                    e.wait_ge(s["wrt"], b["wrt"] + 16 * wrt_at[i - 1])
                    e.wait_ge(s["sq"], b["sq"] + sq_at[i - 1])
                e.tensor_copy(stage_of(i), X).then_inc(s["ev"], 1)
            else:
                e.tensor_copy(stage_of(i), X).then_inc(s["ev"], 1)
                st0 = C.stage[:, 0:2048]
                st1 = C.stage[:, 2048:4096]
                e.wait_ge(s["sq"], b["sq"] + sq_at[i])
                e.tensor_mul(C.tmp0, st0, C.cosb)
                e.tensor_mul(C.tmp1, st1, C.sinb)
                e.tensor_sub(C.tmp0, C.tmp0, C.tmp1)
                e.tensor_mul(C.tmp1, st0, C.sinb)
                e.tensor_mul(st1, st1, C.cosb)
                e.tensor_add(st1, st1, C.tmp1).then_inc(s["rp"], 1)
                npair2 += 1
            if _NORMED[kind]:
                e.wait_ge(s["nm"], b["nm"] + nm_at[i])
                mxcol = {"qn": idx, "qr1": 16 + idx, "qr2": 20 + idx,
                         "kn": 24 + idx, "kr1": 28, "kr2": 29}[kind]
                XB = C.pA if sch["psum_name"][i] == "A" else C.pB
                e.reduce_max(C.mx[0:1, mxcol:mxcol + 1], XB[0:1, 0:2048], AXX
                             ).then_inc(s["nr"], 1)
        _p2_negb(eng, e, C, b, batch, tot)

    if eng == "scalar":
        jn = 0
        for i, (kind, idx) in enumerate(chunks):
            if not _NORMED[kind]:
                continue
            jn += 1
            if jn >= 3:
                e.wait_ge(s["nm"], b["nm"] + jn - 2)
            e.wait_ge(s["ev"], b["ev"] + ev_at[i])
            e.activation(sq_slot(i), stage_of(i), AF.Square).then_inc(s["sq"], 1)
        _p2_negb(eng, e, C, b, batch, tot)

    if eng == "sync":
        npair2 = 0
        for i, (kind, idx) in enumerate(chunks):
            if kind == "qn":
                e.wait_ge(s["ev"], b["ev"] + ev_at[i])
                rows = (batch * H + idx) * 128
                e.dma_start(t["qn_scr"][rows: rows + 128, :], stage_of(i)).then_inc(s["wrt"], 16)
            elif kind == "kn":
                e.wait_ge(s["ev"], b["ev"] + ev_at[i])
                rows = (batch * KV + idx) * 128
                e.dma_start(t["kn_scr"][rows: rows + 128, :], stage_of(i)).then_inc(s["wrt"], 16)
            elif kind == "v" and idx in (7, 15):
                e.wait_ge(s["ev"], b["ev"] + ev_at[i])
                rows = (batch * 2 + idx // 8) * 128
                e.dma_start(t["v_scr"][rows: rows + 128, :], C.stage[:, 0:4096]
                            ).then_inc(s["wrt"], 16)
            elif kind in _PAIR2:
                npair2 += 1
                e.wait_ge(s["rp"], b["rp"] + npair2)
                if kind == "qr2":
                    for hh in range(4):
                        h = idx * 4 + hh
                        rows = (batch * H + h) * RD
                        e.dma_start(t["qr_scr"][rows: rows + 32, :],
                                    C.tmp0[hh * 32:(hh + 1) * 32, :]).then_inc(s["wrt"], 16)
                        e.dma_start(t["qr_scr"][rows + 32: rows + 64, :],
                                    C.stage[hh * 32:(hh + 1) * 32, 2048:4096]).then_inc(s["wrt"], 16)
                else:
                    for g in range(KV):
                        rows = (batch * KV + g) * RD
                        e.dma_start(t["kr_scr"][rows: rows + 32, :],
                                    C.tmp0[g * 32:(g + 1) * 32, :]).then_inc(s["wrt"], 16)
                        e.dma_start(t["kr_scr"][rows + 32: rows + 64, :],
                                    C.stage[g * 32:(g + 1) * 32, 2048:4096]).then_inc(s["wrt"], 16)
        _p2_negb(eng, e, C, b, batch, tot)

    b["aw"] += 16 * (16 + 16 + 16 + 4 + 4 + 4)
    b["mm"] += tot["mm"]
    b["ev"] += tot["ev"]
    b["sq"] += tot["sq"]
    b["nm"] += tot["nm"] + 1
    b["nr"] += tot["nr"] + 1
    b["rp"] += 5 + 15
    b["wrt"] += 16 * tot["wrt"]
    if batch == 0:
        b["cv"] += CONV_CV
        b["cvw"] += CONV_CVW
    return b


def _p2_negb(eng, e, C, b, batch, tot):
    s = C.s
    rp0 = b["rp"] + 5
    if eng == "vector":
        e.wait_ge(s["nr"], b["nr"] + tot["nr"])
        k = rp0

        def step(f):
            nonlocal k
            e.wait_ge(s["rp"], k)
            f().then_inc(s["rp"], 1)
            k += 1

        for c in range(4):
            step(lambda c=c: e.tensor_scalar(C.mx[0:1, 32 + 4 * c: 36 + 4 * c],
                                             C.mx[0:1, 4 * c: 4 * c + 4],
                                             C.mx[0:1, 16 + c: 17 + c], None, ALU.add))
        for c in range(4):
            step(lambda c=c: e.tensor_scalar(C.mx[0:1, 32 + 4 * c: 36 + 4 * c],
                                             C.mx[0:1, 32 + 4 * c: 36 + 4 * c],
                                             C.mx[0:1, 20 + c: 21 + c], None, ALU.add))
        step(lambda: e.tensor_scalar(C.mx[0:1, 48:52], C.mx[0:1, 24:28],
                                     C.mx[0:1, 28:29], None, ALU.add))
        step(lambda: e.tensor_scalar(C.mx[0:1, 48:52], C.mx[0:1, 48:52],
                                     C.mx[0:1, 29:30], None, ALU.add))
        for c in range(4):
            step(lambda c=c: e.tensor_scalar(C.mx[0:1, 32 + 4 * c: 36 + 4 * c],
                                             C.mx[0:1, 32 + 4 * c: 36 + 4 * c],
                                             C.mx[0:1, 48 + c: 49 + c], None, ALU.mult))
        e.wait_ge(s["rp"], k + 1)
        e.wait_ge(s["nm"], b["nm"] + tot["nm"] + 1)
        e.tensor_copy(C.negB[:, batch * H:(batch + 1) * H], C.pA[:, 0:16]
                      ).then_inc(s["nr"], 1)
    if eng == "scalar":
        e.wait_ge(s["rp"], rp0 + 14)
        e.activation(C.mx[0:1, 48:64], C.mx[0:1, 32:48], AF.Sqrt)
        e.mul(C.mx[0:1, 48:64], C.mx[0:1, 48:64], -SCALE).then_inc(s["rp"], 1)
    if eng == "tensor":
        e.wait_ge(s["rp"], rp0 + 15)
        e.matmul(C.pA[:, 0:16], C.onesrow, C.mx[0:1, 48:64], start=True, stop=True
                 ).then_inc(s["nm"], 1)


CONV_CV = 96 * 16
CONV_CVW = 96 * 16


def _emit_conversion(e, C, b):
    s = C.s
    t = C.t
    k = 0
    TOT_EV = 50
    for (srcn, dstn) in (("wg32", "wg_scr"), ("wu32", "wu_scr")):
        for rb in range(16):
            for half in range(2):
                slot = k % 2
                e.wait_ge(s["ev"], b["ev"] + min(TOT_EV, (k * TOT_EV) // 96))
                if k >= 2:
                    e.wait_ge(s["cvw"], b["cvw"] + 16 * (k - 1))
                e.dma_start(C.convb[:, slot * 4096:(slot + 1) * 4096],
                            t[srcn][rb * 128:(rb + 1) * 128, half * 4096:(half + 1) * 4096]
                            ).then_inc(s["cv"], 16)
                e.wait_ge(s["cv"], b["cv"] + 16 * (k + 1))
                e.dma_start(t[dstn][rb * 128:(rb + 1) * 128, half * 4096:(half + 1) * 4096],
                            C.convb[:, slot * 4096:(slot + 1) * 4096]).then_inc(s["cvw"], 16)
                k += 1
    for blk in range(32):
        slot = k % 2
        e.wait_ge(s["ev"], b["ev"] + min(TOT_EV, (k * TOT_EV) // 96))
        e.wait_ge(s["cvw"], b["cvw"] + 16 * (k - 1))
        e.dma_start(C.convb[:, slot * 4096:(slot + 1) * 4096],
                    t["wd32"][blk * 512:(blk + 1) * 512, :].rearrange("(p c) n -> p (c n)", c=4)
                    ).then_inc(s["cv"], 16)
        e.wait_ge(s["cv"], b["cv"] + 16 * (k + 1))
        e.dma_start(t["wd_scr"][blk * 512:(blk + 1) * 512, :].rearrange("(p c) n -> p (c n)", c=4),
                    C.convb[:, slot * 4096:(slot + 1) * 4096]).then_inc(s["cvw"], 16)
        k += 1



def _emit_ffn(eng, e, C, b, batch):
    s = C.s
    t = C.t
    _gate(e, C, b, GATE_FFN)
    for blk in range(NBLK):
        _emit_ffn_block(eng, e, C, b, batch, blk)
        for k, d in (("w", 32 * NQUAD), ("wf", NQUAD), ("g", NQUAD), ("sil", NQUAD),
                     ("ubc", NQUAD), ("hm", NQUAD), ("hq", NQUAD),
                     ("wd", 16 * 2 * NGRP), ("wdf", 2 * NGRP), ("ye", 2),
                     ("xs", 16), ("yo", 16 * 4)):
            b[k] += d
    return b


def _emit_ffn_block(eng, e, C, b, batch, blk):
    s = C.s
    t = C.t
    pg, pu = C.pA, C.pB
    first = (batch == 0 and blk == 0)

    if eng == "gpsimd":
        if not first:
            e.wait_ge(s["wf"], b["wf"])
        rows = (batch * NBLK + blk) * 128
        e.dma_start(C.xnslot, t["xnT_scr"][rows: rows + 128, :]).then_inc(s["xs"], 16)
        e.wait_ge(s["ye"], b["ye"] + 2)
        for tokb in range(4):
            rows = (batch * 16 + blk * 4 + tokb) * 128
            e.dma_start(t["yf_scr"][rows: rows + 128, :],
                        C.ystage[:, tokb * 2048:(tokb + 1) * 2048]).then_inc(s["yo"], 16)

    if eng == "sync":
        for q in range(NQUAD):
            slot = q % 2
            prev_user = b["wf"] + q - 2
            if prev_user >= 0:
                e.wait_ge(s["wf"], prev_user + 1)
            cbase = slot * 2 * NH * 512
            e.dma_start(C.wgwu[:, cbase: cbase + NH * 512],
                        t["wg_scr"][q * 128:(q + 1) * 128, :]).then_inc(s["w"], 16)
            e.dma_start(C.wgwu[:, cbase + NH * 512: cbase + 2 * NH * 512],
                        t["wu_scr"][q * 128:(q + 1) * 128, :]).then_inc(s["w"], 16)
        for pi in range(2):
            for grp in range(NGRP):
                i = pi * NGRP + grp
                slot = i % 2
                prev_user = b["wdf"] + i - 2
                if prev_user >= 0:
                    e.wait_ge(s["wdf"], prev_user + 1)
                r0 = (pi * 64 + grp * WD_GRP) * 128
                e.dma_start(C.wdslot[:, slot * WD_GRP * 1024:(slot + 1) * WD_GRP * 1024]
                            .rearrange("p (c n) -> p c n", c=WD_GRP),
                            t["wd_scr"][r0: r0 + WD_GRP * 128, :].rearrange("(c p) n -> p c n", p=128)
                            ).then_inc(s["wd"], 16)

    if eng == "tensor":
        e.wait_ge(s["xs"], b["xs"] + 16)
        for q in range(NQUAD):
            slot = q % 2
            cbase = slot * 2 * NH * 512
            e.wait_ge(s["w"], b["w"] + 32 * (q + 1))
            if b["sil"] + q >= 1:
                e.wait_ge(s["sil"], b["sil"] + q)
            if q == 0 and b["ye"] >= 2:
                e.wait_ge(s["ye"], b["ye"])
            for mc in range(4):
                for hc in range(NH):
                    ins = e.matmul(pg[:, mc * 512:(mc + 1) * 512],
                                   C.wgwu[:, cbase + hc * 512 + mc * 128: cbase + hc * 512 + mc * 128 + 128],
                                   C.xnslot[:, hc * 512:(hc + 1) * 512],
                                   start=(hc == 0), stop=(hc == NH - 1))
            ins.then_inc(s["g"], 1)
            if b["ubc"] + q >= 1:
                e.wait_ge(s["ubc"], b["ubc"] + q)
            for mc in range(4):
                for hc in range(NH):
                    ins = e.matmul(pu[:, mc * 512:(mc + 1) * 512],
                                   C.wgwu[:, cbase + NH * 512 + hc * 512 + mc * 128: cbase + NH * 512 + hc * 512 + mc * 128 + 128],
                                   C.xnslot[:, hc * 512:(hc + 1) * 512],
                                   start=(hc == 0), stop=(hc == NH - 1))
            ins.then_inc(s["wf"], 1)
        e.wait_ge(s["hq"], b["hq"] + NQUAD)
        e.wait_ge(s["sil"], b["sil"] + NQUAD)
        e.wait_ge(s["ubc"], b["ubc"] + NQUAD)
        for pi in range(2):
            if pi == 1:
                e.wait_ge(s["ye"], b["ye"] + 1)
            for grp in range(NGRP):
                i = pi * NGRP + grp
                slot = i % 2
                e.wait_ge(s["wd"], b["wd"] + 16 * (i + 1))
                for ci in range(WD_GRP):
                    Ic = grp * WD_GRP + ci
                    for tokb in range(4):
                        for hgi in range(2):
                            pt = pg if tokb < 2 else pu
                            bank = (tokb % 2) * 2 + hgi
                            ins = e.matmul(
                                pt[:, bank * 512:(bank + 1) * 512],
                                C.h_sp[:, Ic * 512 + tokb * 128: Ic * 512 + tokb * 128 + 128],
                                C.wdslot[:, slot * WD_GRP * 1024 + ci * 1024 + hgi * 512:
                                         slot * WD_GRP * 1024 + ci * 1024 + hgi * 512 + 512],
                                start=(Ic == 0), stop=(Ic == 63))
                ins.then_inc(s["wdf"], 1)

    if eng == "scalar":
        for q in range(NQUAD):
            e.wait_ge(s["g"], b["g"] + q + 1)
            if b["hm"] + q >= 1:
                e.wait_ge(s["hm"], b["hm"] + q)
            for mc in range(4):
                ins = e.activation(C.sg[:, mc * 512:(mc + 1) * 512],
                                   pg[:, mc * 512:(mc + 1) * 512], AF.Silu)
            ins.then_inc(s["sil"], 1)
        for pi in range(2):
            e.wait_ge(s["wdf"], b["wdf"] + (pi + 1) * NGRP)
            if pi == 0 and b["yo"] > 0:
                e.wait_ge(s["yo"], b["yo"])
            for tokb in range(4):
                for hgi in range(2):
                    pt = pg if tokb < 2 else pu
                    bank = (tokb % 2) * 2 + hgi
                    hg = pi * 2 + hgi
                    ins = e.copy(C.ystage[:, tokb * 2048 + hg * 512: tokb * 2048 + hg * 512 + 512],
                                 pt[:, bank * 512:(bank + 1) * 512])
            ins.then_inc(s["ye"], 1)

    if eng == "vector":
        for q in range(NQUAD):
            if q == 0 and b["wdf"] >= 1:
                e.wait_ge(s["wdf"], b["wdf"])
            e.wait_ge(s["wf"], b["wf"] + q + 1)
            for mc in range(4):
                ins = e.tensor_copy(C.ub[:, mc * 512:(mc + 1) * 512],
                                    pu[:, mc * 512:(mc + 1) * 512])
            ins.then_inc(s["ubc"], 1)
            e.wait_ge(s["sil"], b["sil"] + q + 1)
            hs = [C.h_sp[:, (q * 4 + m) * 512:(q * 4 + m + 1) * 512] for m in range(4)]
            for m in range(4):
                ins = e.tensor_mul(hs[m], C.sg[:, m * 512:(m + 1) * 512],
                                   C.ub[:, m * 512:(m + 1) * 512])
            ins.then_inc(s["hm"], 1)
            sqs = [C.sq4[:, m * 512:(m + 1) * 512] for m in range(4)]
            for m in range(4):
                e.tensor_mul(sqs[m], hs[m], hs[m])
            cmp0 = C.cmp2[:, 0:512]
            cmp1 = C.cmp2[:, 512:1024]
            wins = [C.ub[:, m * 512:(m + 1) * 512] for m in range(4)]
            for m in range(4):
                ops = []
                for j in range(4):
                    if j == m:
                        continue
                    op = ALU.is_ge if m < j else ALU.is_gt
                    ops.append((j, op))
                e.tensor_tensor(cmp0, sqs[m], sqs[ops[0][0]], ops[0][1])
                e.tensor_tensor(cmp1, sqs[m], sqs[ops[1][0]], ops[1][1])
                e.tensor_add(wins[m], cmp0, cmp1)
                e.tensor_tensor(cmp0, sqs[m], sqs[ops[2][0]], ops[2][1])
                e.tensor_add(wins[m], wins[m], cmp0)
            for m in range(4):
                ins = e.scalar_tensor_tensor(hs[m], wins[m], 1.5, hs[m], ALU.is_ge, ALU.mult)
            ins.then_inc(s["hq"], 1)



def _atta_sched():
    tiles = []
    head_end = {}
    group_end = {}
    hqb_last = {}
    g = 0
    for h in range(H):
        for qb in range(4):
            for kt in range(4 * qb + 4):
                masked = kt >= 4 * qb
                tiles.append((h, qb, kt, masked, g))
                hqb_last[(h, qb)] = g
                g += 1
        head_end[h] = g
        if h % 4 == 3:
            group_end[h // 4] = g
    em_at = {}
    av_at = {}
    cem = 0
    for (h, qb, kt, masked, gi) in tiles:
        if masked:
            cem += 1
        em_at[gi] = cem
        av_at[gi] = gi + 1
    return tiles, head_end, group_end, hqb_last, em_at


_ATILES, _AHEND, _AGEND, _AHQBL, _AEM = _atta_sched()
NTILE = len(_ATILES)
NMASK = _AEM[_ATILES[-1][4]]


def _emit_atta(eng, e, C, b, batch):
    s = C.s
    t = C.t
    _gate(e, C, b, GATE_ATTA)
    tiles = _ATILES

    if eng == "sync":
        for j in range(2):
            rows = (batch * 2 + j) * 128
            e.dma_start(C.vsb[:, j * 4096:(j + 1) * 4096],
                        t["v_scr"][rows: rows + 128, :]).then_inc(s["kv"], 16)
        for h in range(H):
            g = h // 4
            if h % 4 == 0:
                if g >= 2:
                    e.wait_ge(s["sc"], b["sc"] + _AGEND[g - 2])
                rows = (batch * KV + g) * 128
                e.dma_start(C.karena[:, (g % 2) * 2048:((g % 2) + 1) * 2048],
                            t["kn_scr"][rows: rows + 128, :]).then_inc(s["kv"], 16)
                rows = (batch * KV + g) * RD
                e.dma_start(C.krarena[0:64, (g % 2) * 2048:((g % 2) + 1) * 2048],
                            t["kr_scr"][rows: rows + RD, :]).then_inc(s["kv"], 16)
            if h >= 2:
                e.wait_ge(s["sc"], b["sc"] + _AHEND[h - 2])
            rows = (batch * H + h) * 128
            e.dma_start(C.qn_a[:, (h % 2) * 2048:((h % 2) + 1) * 2048],
                        t["qn_scr"][rows: rows + 128, :]).then_inc(s["qh"], 16)
            rows = (batch * H + h) * RD
            e.dma_start(C.qr_a[0:64, (h % 2) * 2048:((h % 2) + 1) * 2048],
                        t["qr_scr"][rows: rows + RD, :]).then_inc(s["qh"], 16)

    if eng == "tensor":
        pend = []
        prev = None
        nhqb = 0

        def avz(pt):
            h, qb, kt, masked, gi, n = pt
            g = h // 4
            slot = gi % 2
            e.wait_ge(s["ex"], b["ex"] + gi + 1)
            if masked:
                e.wait_ge(s["em"], b["em"] + _AEM[gi])
            if kt == 0 and n >= 1:
                e.wait_ge(s["on"], b["on"] + n - 1)
                e.wait_ge(s["zr"], b["zr"] + 2 * (n - 1) + 1)
            E = C.esb[:, slot * 512:(slot + 1) * 512]
            pO = C.pO[:, (n % 2) * 512:((n % 2) + 1) * 512]
            last = (gi == _AHQBL[(h, qb)])
            e.matmul(pO, C.vsb[:, kt * 512 + (g % 4) * 128: kt * 512 + (g % 4) * 128 + 128],
                     E, start=(kt == 0), stop=last)
            e.matmul(C.pZ[0:1, 0:512], C.onescol, E,
                     start=(kt == 0), stop=last).then_inc(s["av"], 1)

        for (h, qb, kt, masked, gi) in tiles:
            g = h // 4
            slot = gi % 2
            pS = C.pS[:, slot * 512:(slot + 1) * 512]
            if kt == 0:
                e.wait_ge(s["kv"], b["kv"] + 16 * (2 + 2 * (g + 1)))
                e.wait_ge(s["qh"], b["qh"] + 16 * 2 * (h + 1))
            if gi >= 2:
                e.wait_ge(s["ex"], b["ex"] + gi - 1)
            e.matmul(pS, C.karena[:, (g % 2) * 2048 + kt * 128:(g % 2) * 2048 + kt * 128 + 128],
                     C.qn_a[:, (h % 2) * 2048 + qb * 512:(h % 2) * 2048 + (qb + 1) * 512],
                     start=True, stop=False)
            e.matmul(pS, C.krarena[:, (g % 2) * 2048 + kt * 128:(g % 2) * 2048 + kt * 128 + 128],
                     C.qr_a[:, (h % 2) * 2048 + qb * 512:(h % 2) * 2048 + (qb + 1) * 512],
                     start=False, stop=True).then_inc(s["sc"], 1)
            if pend:
                n = pend.pop(0)
                e.wait_ge(s["zr"], b["zr"] + 2 * n + 1)
                e.wait_ge(s["on"], b["on"] + n)
                e.matmul(C.pBC, C.onesrowb, C.rowbufb[0:1, 0:512],
                         start=True, stop=True).then_inc(s["zr"], 1)
            if prev is not None:
                avz(prev)
                if prev[4] == _AHQBL[(prev[0], prev[1])]:
                    pend.append(prev[5])
            prev = (h, qb, kt, masked, gi, nhqb)
            if gi == _AHQBL[(h, qb)]:
                nhqb += 1
        avz(prev)
        pend.append(prev[5])
        for n in pend:
            e.wait_ge(s["zr"], b["zr"] + 2 * n + 1)
            e.wait_ge(s["on"], b["on"] + n)
            e.matmul(C.pBC, C.onesrowb, C.rowbufb[0:1, 0:512],
                     start=True, stop=True).then_inc(s["zr"], 1)

    if eng == "scalar":
        for (h, qb, kt, masked, gi) in tiles:
            e.wait_ge(s["sc"], b["sc"] + gi + 1)
            if gi >= 2:
                e.wait_ge(s["av"], b["av"] + gi - 1)
            slot = gi % 2
            e.activation(C.esb[:, slot * 512:(slot + 1) * 512],
                         C.pS[:, slot * 512:(slot + 1) * 512], AF.Exp,
                         bias=C.negB[:, batch * H + h: batch * H + h + 1],
                         scale=SCALE).then_inc(s["ex"], 1)

    if eng == "vector":
        e.memset(C.krarena[64:128, 0:2048], 0.0)
        e.memset(C.krarena[64:128, 2048:4096], 0.0).then_inc(s["lns"], 2)
        nhqb = 0
        for (h, qb, kt, masked, gi) in tiles:
            slot = gi % 2
            if masked:
                e.wait_ge(s["ex"], b["ex"] + gi + 1)
                dtile = kt - 4 * qb
                e.tensor_mul(C.esb[:, slot * 512:(slot + 1) * 512],
                             C.esb[:, slot * 512:(slot + 1) * 512],
                             C.bmask[:, dtile * 512:(dtile + 1) * 512]).then_inc(s["em"], 1)
            if gi == _AHQBL[(h, qb)]:
                e.wait_ge(s["av"], b["av"] + gi + 1)
                e.reciprocal(C.rowbuf[0:1, 0:512], C.pZ[0:1, 0:512])
                e.tensor_copy(C.rowbufb[0:1, 0:512], C.rowbuf[0:1, 0:512]).then_inc(s["zr"], 1)
                e.wait_ge(s["zr"], b["zr"] + 2 * nhqb + 2)
                e.tensor_copy(C.rbsb, C.pBC)
                e.tensor_tensor(C.obuf[:, h * 2048 + qb * 512: h * 2048 + (qb + 1) * 512],
                                C.pO[:, (nhqb % 2) * 512:((nhqb % 2) + 1) * 512],
                                C.rbsb, ALU.mult).then_inc(s["on"], 1)
                nhqb += 1

    for k, d in (("kv", 16 * 10), ("qh", 16 * 32), ("sc", NTILE), ("ex", NTILE),
                 ("em", NMASK), ("av", NTILE), ("zr", 2 * 64), ("on", 64), ("lns", 2)):
        b[k] += d
    return b



def _emit_attb(eng, e, C, b, batch):
    s = C.s
    t = C.t
    _gate(e, C, b, GATE_ATTB)

    if eng == "gpsimd":
        for hh in range(H):
            e.dma_start(C.wosb[:, hh * 2048:(hh + 1) * 2048],
                        t["won"][hh * 128:(hh + 1) * 128, :]).then_inc(s["wol"], 16)

    if eng == "sync":
        for w in range(8):
            for tt in range(2):
                tokb = w * 2 + tt
                if tokb >= 2:
                    e.wait_ge(s["yq"], b["yq"] + tokb - 1)
                rows = (batch * 16 + tokb) * 128
                e.dma_start(C.yfslot[:, (tokb % 2) * 2048:((tokb % 2) + 1) * 2048],
                            t["yf_scr"][rows: rows + 128, :]).then_inc(s["yw"], 16)
                e.dma_start(C.xtile[:, (tokb % 2) * 2048:((tokb % 2) + 1) * 2048],
                            t["x"][batch * S + tokb * 128: batch * S + (tokb + 1) * 128, :]
                            ).then_inc(s["yw"], 16)
            for tt in range(2):
                tokb = w * 2 + tt
                e.wait_ge(s["yq"], b["yq"] + tokb + 1)
                e.dma_start(t["out"][batch * S + tokb * 128: batch * S + (tokb + 1) * 128, :],
                            C.ostage[:, tt * 2048:(tt + 1) * 2048]).then_inc(s["outd"], 16)

    if eng == "tensor":
        e.wait_ge(s["wol"], b["wol"] + 16 * H)
        for w in range(8):
            if w >= 1:
                e.wait_ge(s["yq"], b["yq"] + 2 * w)
            for hh in range(H):
                for tt in range(2):
                    pt = C.pA if tt == 0 else C.pB
                    for hg in range(4):
                        ins = e.matmul(
                            pt[:, hg * 512:(hg + 1) * 512],
                            C.obuf[:, hh * 2048 + w * 256 + tt * 128: hh * 2048 + w * 256 + tt * 128 + 128],
                            C.wosb[:, hh * 2048 + hg * 512: hh * 2048 + (hg + 1) * 512],
                            start=(hh == 0), stop=(hh == H - 1))
            ins.then_inc(s["wo"], 1)

    if eng == "vector":
        for w in range(8):
            e.wait_ge(s["wo"], b["wo"] + w + 1)
            for tt in range(2):
                tokb = w * 2 + tt
                pt = C.pA if tt == 0 else C.pB
                e.wait_ge(s["yw"], b["yw"] + 32 * (tokb + 1))
                if w >= 1:
                    e.wait_ge(s["outd"], b["outd"] + 16 * (2 * (w - 1) + tt + 1))
                e.tensor_add(C.ostage[:, tt * 2048:(tt + 1) * 2048], pt[:, 0:2048],
                             C.yfslot[:, (tokb % 2) * 2048:((tokb % 2) + 1) * 2048])
                e.tensor_add(C.ostage[:, tt * 2048:(tt + 1) * 2048],
                             C.ostage[:, tt * 2048:(tt + 1) * 2048],
                             C.xtile[:, (tokb % 2) * 2048:((tokb % 2) + 1) * 2048]
                             ).then_inc(s["yq"], 1)

    for k, d in (("wol", 16 * H), ("yw", 16 * 32), ("wo", 8), ("yq", 16),
                 ("outd", 16 * 16)):
        b[k] += d
    return b



def _build_program():
    from contextlib import ExitStack
    nc = bass.Bass("TRN2", target_bir_lowering=False)
    stack = ExitStack()
    with stack:
        C = _declare(nc, stack)
        _alloc_overlays(nc, C)
        with nc.Block() as block:
            def body(eng):
                def f(e):
                    s = C.s
                    if eng == "gpsimd":
                        for ap, src in ((C.identb, C.t["identb"]), (C.identf, C.t["identf"]),
                                        (C.onescol, C.t["onescol"]), (C.onesrow, C.t["onesrow"]),
                                        (C.onesrowb, C.t["onesrowb"]),
                                        (C.cosb, C.t["cosb"]), (C.sinb, C.t["sinb"]),
                                        (C.bmask, C.t["bmask"])):
                            e.dma_start(ap, src).then_inc(s["cv"], 16)
                        e.memset(C.epsap, EPS).then_inc(s["cv"], 1)
                    b = _zero_counts()
                    b["cv"] = 8 * 16 + 1
                    if eng in ("tensor", "vector", "scalar"):
                        e.wait_ge(s["cv"], 8 * 16 + 1)
                    for batch in range(BPC):
                        if "p1" in PHASES:
                            b = _emit_p1(eng, e, C, b, batch)
                        if "p2" in PHASES:
                            b = _emit_p2(eng, e, C, b, batch)
                        if "ffn" in PHASES:
                            b = _emit_ffn(eng, e, C, b, batch)
                        if "att" in PHASES:
                            b = _emit_atta(eng, e, C, b, batch)
                            b = _emit_attb(eng, e, C, b, batch)
                    if eng == "gpsimd":
                        for n in ("xnw", "wrt", "cvw", "yo", "outd"):
                            if b[n] > 0:
                                e.wait_ge(s[n], b[n])
                return f
            block.gpsimd(body("gpsimd"))
            block.tensor(body("tensor"))
            block.vector(body("vector"))
            block.scalar(body("scalar"))
            block.sync(body("sync"))
    return nc


def _get_nc():
    if "nc" not in _CACHE:
        _CACHE["nc"] = _build_program()
    return _CACHE["nc"]


def kernel(**inputs):
    in_maps = _host_prep(inputs)
    nc = _get_nc()
    res = run_bass_kernel_spmd(nc, in_maps, core_ids=list(range(NCORES)))
    out = np.empty((B_ALL, S, D), np.float32)
    for cid in range(NCORES):
        out[BPC * cid: BPC * (cid + 1)] = res.results[cid]["out"].reshape(BPC, S, D)
    if DEBUG_DUMPS:
        kernel.last_results = res.results
    return out


# version 09
# speedup vs baseline: 1.2098x; 1.0366x over previous
import math
import numpy as np
import ml_dtypes
import concourse.bass as bass
import concourse.mybir as mybir
from concourse.bass_utils import run_bass_kernel_spmd

F32 = mybir.dt.float32
BF16 = mybir.dt.bfloat16
AF = mybir.ActivationFunctionType
ALU = mybir.AluOpType
AXX = mybir.AxisListType.X
BF = ml_dtypes.bfloat16

B_ALL, S, D = 16, 2048, 2048
NCORES = 8
BPC = B_ALL // NCORES
TOKS = BPC * S
NH = D // 128
H, KV, HD, RD = 16, 4, 128, 64
NR2 = RD // 2
IFF = 8192
NQUAD = 16
WD_GRP = 4
NGRP = 64 // WD_GRP
SCALE = 1.0 / math.sqrt(HD + RD)
NBLK = S // 512
EPS = 1e-5

DEBUG_DUMPS = ()
PHASES = ("p1", "p2", "ffn", "att")

_CACHE = {}



def _ffn_perm():
    q, m, p = np.meshgrid(np.arange(16), np.arange(4), np.arange(128), indexing="ij")
    perm = np.empty(IFF, np.int64)
    perm[(q * 512 + m * 128 + p).ravel()] = (4 * (q * 128 + p) + m).ravel()
    return perm


def _rope_perm(nheads):
    cols = []
    for x2 in (0, 1):
        for h in range(nheads):
            for f in range(NR2):
                cols.append(h * RD + 2 * f + x2)
    return np.asarray(cols, np.int64)


def _make_wg_scr(w):
    return np.ascontiguousarray(w.reshape(16, 128, 16, 512).transpose(2, 1, 0, 3).reshape(2048, 8192))


def _make_wd_scr(w):
    return np.ascontiguousarray(w.reshape(64, 128, 2, 1024).transpose(2, 0, 1, 3).reshape(16384, 1024))


def _host_consts():
    c = {}
    c["identb"] = np.eye(128, dtype=BF)
    c["identf"] = np.eye(128, dtype=np.float32)
    c["onescol"] = np.ones((128, 1), dtype=BF)
    c["onesrow"] = np.ones((1, 128), dtype=np.float32)
    c["onesrowb"] = np.ones((1, 128), dtype=BF)
    freqs = 1.0 / (10000.0 ** (np.arange(0, RD, 2, dtype=np.float64) / RD))
    t = np.arange(S, dtype=np.float64)
    ang = np.outer(freqs, t)
    c["cosb"] = np.tile(np.cos(ang), (4, 1)).astype(BF)
    c["sinb"] = np.tile(np.sin(ang), (4, 1)).astype(BF)
    kk = np.arange(128)[:, None]
    qq = np.arange(512)[None, :]
    c["bmask"] = np.ascontiguousarray(
        np.concatenate([(qq >= kk + d).astype(BF) for d in (0, 128, 256, 384)], axis=1))
    return c


def _host_prep(inputs):
    x = np.asarray(inputs["L0_attn_hidden_states"])
    fperm = _ffn_perm()
    Wo = np.asarray(inputs["L0_attn_Wo"])
    shared = {
        "wq": np.ascontiguousarray(np.asarray(inputs["L0_attn_Wq"])),
        "wqr": np.ascontiguousarray(np.asarray(inputs["L0_attn_Wqrope"])[:, _rope_perm(H)]),
        "wkvc": np.ascontiguousarray(np.asarray(inputs["L0_attn_Wkvc"])),
        "wkdec": np.ascontiguousarray(np.asarray(inputs["L0_attn_Wkdec"])),
        "wvdec": np.ascontiguousarray(np.asarray(inputs["L0_attn_Wvdec"])),
        "wkr": np.ascontiguousarray(np.asarray(inputs["L0_attn_Wkrope"])[:, _rope_perm(KV)]),
        "won": np.ascontiguousarray(Wo.reshape(H, HD + RD, D)[:, :HD].reshape(H * HD, D)),
        "wg32": _make_wg_scr(np.ascontiguousarray(np.asarray(inputs["L0_ffn_w_gate"])[:, fperm])),
        "wu32": _make_wg_scr(np.ascontiguousarray(np.asarray(inputs["L0_ffn_w_up"])[:, fperm])),
        "wd32": _make_wd_scr(np.ascontiguousarray(np.asarray(inputs["L0_ffn_w_down"])[fperm, :])),
    }
    shared.update(_host_consts())
    in_maps = []
    for cid in range(NCORES):
        m = dict(shared)
        m["x"] = np.ascontiguousarray(x[BPC * cid: BPC * (cid + 1)].reshape(TOKS, D))
        in_maps.append(m)
    return in_maps



class _Ctx:
    pass


_TENSORS = [
    ("x", [TOKS, D], F32, "ExternalInput"),
    ("wq", [D, D], F32, "ExternalInput"),
    ("wqr", [D, H * RD], F32, "ExternalInput"),
    ("wkvc", [D, 512], F32, "ExternalInput"),
    ("wkdec", [512, 512], F32, "ExternalInput"),
    ("wvdec", [512, 512], F32, "ExternalInput"),
    ("wkr", [512, KV * RD], F32, "ExternalInput"),
    ("won", [H * HD, D], F32, "ExternalInput"),
    ("wg32", [2048, 8192], F32, "ExternalInput"),
    ("wu32", [2048, 8192], F32, "ExternalInput"),
    ("wd32", [16384, 1024], F32, "ExternalInput"),
    ("identb", [128, 128], BF16, "ExternalInput"),
    ("identf", [128, 128], F32, "ExternalInput"),
    ("onescol", [128, 1], BF16, "ExternalInput"),
    ("onesrow", [1, 128], F32, "ExternalInput"),
    ("onesrowb", [1, 128], BF16, "ExternalInput"),
    ("cosb", [128, S], BF16, "ExternalInput"),
    ("sinb", [128, S], BF16, "ExternalInput"),
    ("bmask", [128, 4 * 512], BF16, "ExternalInput"),
    ("out", [TOKS, D], F32, "ExternalOutput"),
    ("xnT_scr", [BPC * NBLK * 128, NH * 512], BF16, "Internal"),
    ("qn_scr", [BPC * H * 128, S], BF16, "Internal"),
    ("qr_scr", [BPC * H * RD, S], BF16, "Internal"),
    ("kn_scr", [BPC * KV * 128, S], BF16, "Internal"),
    ("kr_scr", [BPC * KV * RD, S], BF16, "Internal"),
    ("v_scr", [BPC * 2 * 128, 4096], BF16, "Internal"),
    ("yf_scr", [TOKS, D], BF16, "Internal"),
    ("wg_scr", [2048, 8192], BF16, "Internal"),
    ("wu_scr", [2048, 8192], BF16, "Internal"),
    ("wd_scr", [16384, 1024], BF16, "Internal"),
]

SEM_NAMES = [
    "x", "xf", "sqa", "ln", "sq2", "ln2", "tp", "tpf", "xnw", "lns",
    "cv", "cvw", "cvg",
    "aw", "mm", "ev", "sq", "nm", "nr", "rp", "wrt",
    "w", "wf", "g", "sil", "ubc", "hm", "hq", "wd", "wdf", "ye",
    "yo", "xs",
    "kv", "qh", "sc", "ex", "em", "zr", "on", "wo", "yq", "outd", "wol",
    "av", "yw",
]


def _zero_counts():
    return {n: 0 for n in SEM_NAMES}


def _declare(nc, stack):
    C = _Ctx()
    C.t = {}
    for name, shp, dt, kind in _TENSORS:
        if name in DEBUG_DUMPS and kind == "Internal":
            kind = "ExternalOutput"
        C.t[name] = nc.dram_tensor(name, shp, dt, kind=kind).ap()
    C.s = {n: stack.enter_context(nc.semaphore("s_" + n)) for n in SEM_NAMES}

    sb = lambda name, shp, dt: stack.enter_context(nc.sbuf_tensor(name, shp, dt))
    C.identb = sb("identb_sb", [128, 128], BF16).ap()
    C.identf = sb("identf_sb", [128, 128], F32).ap()
    C.onescol = sb("onescol_sb", [128, 1], BF16).ap()
    C.onesrow = sb("onesrow_sb", [1, 128], F32).ap()
    C.cosb = sb("cosb_sb", [128, S], BF16).ap()
    C.sinb = sb("sinb_sb", [128, S], BF16).ap()
    C.bmask = sb("bmask_sb", [128, 4 * 512], BF16).ap()
    C.negB = sb("negB_sb", [128, 2 * H], F32).ap()
    C.mx = sb("mx_sb", [1, 64], F32).ap()
    C.small = sb("small_sb", [128, 32], F32).ap()
    C.rowbuf = sb("rowbuf", [1, 512], F32).ap()
    C.rowbufb = sb("rowbufb", [1, 512], BF16).ap()
    C.onesrowb = sb("onesrowb_sb", [1, 128], BF16).ap()
    C.epsap = sb("eps_sb", [128, 1], F32).ap()

    C.blkA = sb("blkA", [128, 32768], BF16).ap()
    C.blkB = sb("blkB", [128, 32768], BF16).ap()
    C.stq = sb("stq", [128, 8192], BF16).ap()
    C.xtile = sb("xtile", [128, 4096], F32).ap()
    C.pool = sb("pool", [128, 15360], BF16).ap()
    return C


def _alloc_overlays(nc, C):
    aA = nc.lookup_mloc(C.blkA.tensor).addr
    aB = nc.lookup_mloc(C.blkB.tensor).addr
    aQ = nc.lookup_mloc(C.stq.tensor).addr
    aX = nc.lookup_mloc(C.xtile.tensor).addr
    aP = nc.lookup_mloc(C.pool.tensor).addr
    at = lambda name, shp, dt, off: nc.alloc_sbuf_tensor_at(name, shp, dt, offset=off).ap()

    C.xnT = at("xnT", [128, NH * S], BF16, aA)
    C.warena = at("warena", [128, 32768], BF16, aB)
    C.stage = at("stage", [128, 2 * 2048], BF16, aQ)
    C.sqs = at("sqs", [128, 2 * 2048], BF16, aQ + 8192)

    C.ctb = at("ctb", [128, 4 * 2048], BF16, aX)
    C.tmp0 = at("tmp0", [128, 2048], BF16, aP + 16384)
    C.tmp1 = at("tmp1", [128, 2048], BF16, aP + 20480)

    C.h_sp = at("h_sp", [128, 64 * 512], BF16, aA)
    C.wgwu = at("wgwu", [128, 2 * 2 * NH * 512], BF16, aB)
    C.xnslot = at("xnslot", [128, NH * 512], BF16, aQ)
    C.wdslot = at("wdslot", [128, 2 * WD_GRP * 1024], BF16, aX)
    C.sg = at("sg", [128, 4 * 512], BF16, aP)
    C.ub = at("ub", [128, 4 * 512], BF16, aP + 4096)
    C.sq4 = at("sq4", [128, 4 * 512], BF16, aP + 8192)
    C.cmp2 = at("cmp2", [128, 2 * 512], BF16, aP + 12288)
    C.ystage = at("ystage", [128, 4 * 2048], BF16, aP + 14336)

    C.obuf = at("obuf", [128, H * S], BF16, aB)
    off = aA
    def ata(name, shp, dt):
        nonlocal off
        ap = at(name, shp, dt, off)
        nbytes = int(np.prod(shp[1:])) * (2 if dt == BF16 else 4)
        off += (nbytes + 31) & ~31
        return ap
    C.vsb = ata("vsb", [128, 16 * 512], BF16)
    C.karena = ata("karena", [128, 2 * 2048], BF16)
    C.krarena = ata("krarena", [128, 2 * 2048], BF16)
    C.qn_a = ata("qn_a", [128, 2 * 2048], BF16)
    C.qr_a = ata("qr_a", [128, 2 * 2048], BF16)
    C.esb = ata("esb", [128, 2 * 512], BF16)
    C.rzbuf = ata("rzbuf", [128, 8], F32)
    C.rbsb = ata("rbsb", [128, 512], F32)
    assert off - aA <= 53 * 1024

    C.wosb = at("wosb", [128, H * 2048], BF16, aA)
    C.yfslot = at("yfslot", [128, 2 * 2048], BF16, aP)
    C.ostage = at("ostage", [128, 2 * 2048], F32, aP + 8192)

    C.pA = nc.alloc_psum_tensor("pA", [128, 2048], F32).ap()
    C.pB = nc.alloc_psum_tensor("pB", [128, 2048], F32).ap()
    saved = nc.psum_base
    nc.psum_base = 0
    C.ptp = nc.alloc_psum_tensor("ptp", [128, 2 * 2048], BF16).ap()
    nc.psum_base = 0
    C.pS = nc.alloc_psum_tensor("pS", [128, 2 * 512], F32).ap()
    C.pO = nc.alloc_psum_tensor("pO", [128, 2 * 512], F32).ap()
    C.pZ = nc.alloc_psum_tensor("pZ", [1, 512], F32).ap()
    C.pBC = nc.alloc_psum_tensor("pBC", [128, 512], F32).ap()
    nc.psum_base = saved
    return C


def _gate(e, C, b, names):
    for n in names:
        if b[n] > 0:
            e.wait_ge(C.s[n], b[n])


GATE_P1 = ("wo", "yq", "outd", "wol", "wdf", "hq", "ye", "yo", "xs", "wrt", "nr")
GATE_P2 = ("tp", "tpf", "sq2", "xnw")
GATE_FFN = ("mm", "ev", "sq", "nr", "wrt", "aw", "cvw")
GATE_ATTA = ("wdf", "hq", "ye", "w", "wd", "yo", "xs")
GATE_ATTB = ("on", "ex", "em", "kv", "qh", "zr")



def _emit_p1(eng, e, C, b, batch):
    s = C.s
    t = C.t
    nsub = 16
    _gate(e, C, b, GATE_P1)
    if eng == "sync":
        for i in range(nsub):
            slot = i % 2
            prev = b["ln2"] + i - 2
            if prev >= 0:
                e.wait_ge(s["ln2"], prev + 1)
            e.dma_start(C.xtile[:, slot * 2048:(slot + 1) * 2048],
                        t["x"][batch * S + i * 128: batch * S + (i + 1) * 128, :]
                        ).then_inc(s["x"], 16)
        for blk in range(NBLK):
            e.wait_ge(s["tpf"], b["tpf"] + (blk + 1) * 4)
            rows = (batch * NBLK + blk) * 128
            e.dma_start(t["xnT_scr"][rows: rows + 128, :],
                        C.xnT[:, blk * 8192:(blk + 1) * 8192]).then_inc(s["xnw"], 16)

    if eng == "scalar":
        for i in range(nsub):
            slot = i % 2
            sm = C.small[:, slot * 8: slot * 8 + 8]
            e.wait_ge(s["ln"], b["ln"] + i + 1)
            e.activation(sm[:, 4:5], sm[:, 3:4], AF.Sqrt, bias=C.epsap).then_inc(s["sq2"], 1)

    if eng == "vector":
        if b["xnw"] > 0:
            e.wait_ge(s["xnw"], b["xnw"])

        def evac(j):
            e.wait_ge(s["tp"], b["tp"] + j + 1)
            dst = C.xnT[:, (j // 4) * 8192:(j // 4 + 1) * 8192]
            e.tensor_copy(
                dst.rearrange("p (h t) -> p h t", h=NH)[:, :, (j % 4) * 128:(j % 4 + 1) * 128],
                C.ptp[:, (j % 2) * 2048:((j % 2) + 1) * 2048].rearrange("p (h t) -> p h t", h=NH)
            ).then_inc(s["tpf"], 1)

        for i in range(nsub):
            slot = i % 2
            xap = C.xtile[:, slot * 2048:(slot + 1) * 2048]
            sm = C.small[:, slot * 8: slot * 8 + 8]
            e.wait_ge(s["x"], b["x"] + (i + 1) * 16)
            e.reduce_sum(sm[:, 1:2], xap, AXX).then_inc(s["lns"], 1)
            e.tensor_mul(C.sqs[:, slot * 2048:(slot + 1) * 2048], xap, xap)
            e.reduce_sum(sm[:, 0:1], C.sqs[:, slot * 2048:(slot + 1) * 2048], AXX
                         ).then_inc(s["lns"], 1)
            e.wait_ge(s["lns"], b["lns"] + 5 * i + 2)
            e.tensor_scalar(sm[:, 2:3], sm[:, 1:2], 1.0 / D, None, ALU.mult
                            ).then_inc(s["lns"], 1)
            e.wait_ge(s["lns"], b["lns"] + 5 * i + 3)
            e.tensor_mul(sm[:, 5:6], sm[:, 2:3], sm[:, 2:3]).then_inc(s["lns"], 1)
            e.wait_ge(s["lns"], b["lns"] + 5 * i + 4)
            e.tensor_scalar(sm[:, 3:4], sm[:, 0:1], 1.0 / D, sm[:, 5:6],
                            ALU.mult, ALU.subtract).then_inc(s["ln"], 1)
            e.wait_ge(s["sq2"], b["sq2"] + i + 1)
            e.reciprocal(sm[:, 6:7], sm[:, 4:5]).then_inc(s["lns"], 1)
            if b["tp"] + i - 1 >= 1:
                e.wait_ge(s["tp"], b["tp"] + i - 1)
            e.wait_ge(s["lns"], b["lns"] + 5 * i + 5)
            e.tensor_scalar(C.stage[:, slot * 2048:(slot + 1) * 2048], xap,
                            sm[:, 2:3], sm[:, 6:7], ALU.subtract, ALU.mult
                            ).then_inc(s["ln2"], 1)
            if i >= 1:
                evac(i - 1)
        evac(nsub - 1)

    if eng == "tensor":
        for i in range(nsub):
            slot = i % 2
            e.wait_ge(s["ln2"], b["ln2"] + i + 1)
            if b["tpf"] + i - 1 >= 1:
                e.wait_ge(s["tpf"], b["tpf"] + i - 1)
            for hc in range(NH):
                ins = e.transpose(C.ptp[:, slot * 2048 + hc * 128: slot * 2048 + (hc + 1) * 128],
                                  C.stage[:, slot * 2048 + hc * 128: slot * 2048 + (hc + 1) * 128],
                                  C.identb)
            ins.then_inc(s["tp"], 1)

    for k, d in (("x", 256), ("ln", 16), ("sq2", 16), ("lns", 5 * 16),
                 ("ln2", 16), ("tp", 16), ("tpf", 16), ("xnw", 16 * NBLK)):
        b[k] += d
    return b



def _p2_chunks():
    out = []
    for pr in range(4):
        out.append(("qr1", pr))
        out.append(("qr2", pr))
    for c in range(4):
        out.append(("c", c))
    for g in range(KV):
        out.append(("kn", g))
    out.append(("kr1", 0))
    out.append(("kr2", 0))
    for kt in range(16):
        out.append(("v", kt))
    for h in range(H):
        out.append(("qn", h))
    return out


_PAIR1 = ("qr1", "kr1")
_PAIR2 = ("qr2", "kr2")
_NORMED = {"qn": True, "qr1": True, "qr2": True, "c": False, "kn": True,
           "kr1": True, "kr2": True, "v": False}


def _p2_sched():
    chunks = _p2_chunks()
    nC = len(chunks)
    mm_at, ev_at, sq_at, nm_at, nr_at, wrt_at = {}, {}, {}, {}, {}, {}
    psum_name, stage_slot, nst_at = {}, {}, {}
    cmm = cev = csq = cnm = cwrt = 0
    nbig = 0
    nst = 0
    wrt_of = {"qn": 1, "qr1": 0, "qr2": 8, "c": 0, "kn": 1, "kr1": 0, "kr2": 8, "v": 0}
    for i, (kind, idx) in enumerate(chunks):
        cmm += 1
        mm_at[i] = cmm
        cev += 1
        ev_at[i] = cev
        if _NORMED[kind]:
            csq += 1
            cnm += 1
        sq_at[i] = csq
        nm_at[i] = cnm
        nr_at[i] = cnm
        cwrt += wrt_of[kind]
        if kind == "v" and idx in (7, 15):
            cwrt += 1
        wrt_at[i] = cwrt
        if kind in _PAIR1:
            psum_name[i] = "B"
        elif kind in _PAIR2:
            psum_name[i] = "A"
        elif kind == "v":
            psum_name[i] = "A" if (idx // 4) % 2 == 0 else "B"
        else:
            psum_name[i] = "A" if nbig % 2 == 0 else "B"
            nbig += 1
        if kind in ("qn", "kn"):
            stage_slot[i] = nst % 2
            nst_at[i] = nst
            nst += 1
    tot = dict(mm=cmm, ev=cev, sq=csq, nm=cnm, nr=cnm, wrt=cwrt)
    return chunks, dict(mm_at=mm_at, ev_at=ev_at, sq_at=sq_at, nm_at=nm_at,
                        nr_at=nr_at, wrt_at=wrt_at, psum_name=psum_name,
                        stage_slot=stage_slot, nst_at=nst_at, tot=tot)


_P2C, _P2S = _p2_sched()


def _emit_p2(eng, e, C, b, batch):
    s = C.s
    t = C.t
    if eng == "gpsimd":
        _gate(e, C, b, ("wo",))
    elif eng == "vector":
        e.wait_ge(s["tp"], 16 * (batch + 1))
    elif eng == "tensor":
        _gate(e, C, b, ("yq",))
    chunks, sch = _P2C, _P2S
    nC = len(chunks)
    tot = sch["tot"]
    mm_at, ev_at, sq_at = sch["mm_at"], sch["ev_at"], sch["sq_at"]
    nm_at, nr_at, wrt_at = sch["nm_at"], sch["nr_at"], sch["wrt_at"]

    def psum_of(i):
        base = C.pA if sch["psum_name"][i] == "A" else C.pB
        kind, idx = chunks[i]
        if kind == "v":
            return base[:, (idx % 4) * 512:((idx % 4) + 1) * 512]
        return base

    def prev_user(i):
        for j in range(i - 1, -1, -1):
            if sch["psum_name"][j] == sch["psum_name"][i]:
                return j
        return None

    def stage_of(i):
        kind = chunks[i][0]
        if kind in _PAIR1:
            return C.stage[:, 0:2048]
        if kind in _PAIR2:
            return C.stage[:, 2048:4096]
        sl = sch["stage_slot"][i]
        return C.stage[:, sl * 2048:(sl + 1) * 2048]

    def vstage(kt):
        off = (kt % 8) * 512
        return C.stage[:, off: off + 512]

    def sq_slot(i):
        nsq = sq_at[i]
        return C.sqs[:, ((nsq - 1) % 2) * 2048:(((nsq - 1) % 2) + 1) * 2048]

    def stage_prev(i):
        nst = sch["nst_at"][i]
        if nst < 2:
            return None
        for j in range(i):
            if j in sch["nst_at"] and sch["nst_at"][j] == nst - 2:
                return j
        return None

    def main_mms(kind, idx, X, tpf0=0):
        ins = None
        if kind in ("qn", "qr1", "qr2", "c"):
            if kind == "qn":
                wof = lambda hc: C.warena[:, hc * 2048 + idx * 128: hc * 2048 + idx * 128 + 128]
            elif kind == "c":
                wof = lambda hc: C.warena[:, 16384 + hc * 512 + idx * 128: 16384 + hc * 512 + idx * 128 + 128]
            else:
                cc = idx if kind == "qr1" else 4 + idx
                wof = lambda hc: C.warena[:, hc * 1024 + cc * 128: hc * 1024 + cc * 128 + 128]
            for tb in range(4):
                e.wait_ge(s["tpf"], 16 * batch + 4 * (tb + 1))
                for hc in range(NH):
                    ins = e.matmul(X[:, tb * 512:(tb + 1) * 512], wof(hc),
                                   C.xnT[:, tb * 8192 + hc * 512: tb * 8192 + hc * 512 + 512],
                                   start=(hc == 0), stop=(hc == NH - 1))
        elif kind == "kn":
            for cc in range(4):
                for tb in range(4):
                    ins = e.matmul(X[:, tb * 512:(tb + 1) * 512],
                                   C.warena[:, 24576 + cc * 512 + idx * 128: 24576 + cc * 512 + idx * 128 + 128],
                                   C.ctb[:, cc * 2048 + tb * 512: cc * 2048 + (tb + 1) * 512],
                                   start=(cc == 0), stop=(cc == 3))
        elif kind in ("kr1", "kr2"):
            x0 = 0 if kind == "kr1" else 128
            for cc in range(4):
                for tb in range(4):
                    ins = e.matmul(X[:, tb * 512:(tb + 1) * 512],
                                   C.warena[:, 28672 + cc * 256 + x0: 28672 + cc * 256 + x0 + 128],
                                   C.ctb[:, cc * 2048 + tb * 512: cc * 2048 + (tb + 1) * 512],
                                   start=(cc == 0), stop=(cc == 3))
        elif kind == "v":
            for cc in range(4):
                ins = e.matmul(X,
                               C.ctb[:, cc * 2048 + idx * 128: cc * 2048 + idx * 128 + 128],
                               C.warena[:, 26624 + cc * 512: 26624 + (cc + 1) * 512],
                               start=(cc == 0), stop=(cc == 3))
        return ins

    if eng == "gpsimd":
        for hc in range(NH):
            e.dma_start(C.warena[:, hc * 1024:(hc + 1) * 1024],
                        t["wqr"][hc * 128:(hc + 1) * 128, :]).then_inc(s["aw"], 16)
        for hc in range(NH):
            e.dma_start(C.warena[:, 16384 + hc * 512: 16384 + (hc + 1) * 512],
                        t["wkvc"][hc * 128:(hc + 1) * 128, :]).then_inc(s["aw"], 16)
        for cc in range(4):
            e.dma_start(C.warena[:, 24576 + cc * 512: 24576 + (cc + 1) * 512],
                        t["wkdec"][cc * 128:(cc + 1) * 128, :]).then_inc(s["aw"], 16)
        for cc in range(4):
            e.dma_start(C.warena[:, 26624 + cc * 512: 26624 + (cc + 1) * 512],
                        t["wvdec"][cc * 128:(cc + 1) * 128, :]).then_inc(s["aw"], 16)
        for cc in range(4):
            e.dma_start(C.warena[:, 28672 + cc * 256: 28672 + (cc + 1) * 256],
                        t["wkr"][cc * 128:(cc + 1) * 128, :]).then_inc(s["aw"], 16)
        if batch == 0:
            _emit_conv_range(e, C, 0, 90)
        e.wait_ge(s["mm"], b["mm"] + 8)
        for hc in range(8):
            e.dma_start(C.warena[:, hc * 2048:(hc + 1) * 2048],
                        t["wq"][hc * 128:(hc + 1) * 128, :]).then_inc(s["aw"], 16)
        if batch == 0:
            _emit_conv_range(e, C, 90, 180)
        e.wait_ge(s["mm"], b["mm"] + 34)
        for hc in range(8, NH):
            e.dma_start(C.warena[:, hc * 2048:(hc + 1) * 2048],
                        t["wq"][hc * 128:(hc + 1) * 128, :]).then_inc(s["aw"], 16)
        if batch == 0:
            _emit_conv_range(e, C, 180, len(_CONVP))

    if eng == "tensor":
        AW_SMALL = b["aw"] + 16 * (16 + 16 + 4 + 4 + 4)
        AW_ALL = b["aw"] + 16 * (16 + 16 + 16 + 4 + 4 + 4)
        done_norm = set()

        def norms(j):
            kind = chunks[j][0]
            X = C.pA if sch["psum_name"][j] == "A" else C.pB
            e.wait_ge(s["sq"], b["sq"] + sq_at[j])
            e.wait_ge(s["ev"], b["ev"] + ev_at[j])
            sqap = sq_slot(j)
            for tb in range(4):
                ins = e.matmul(X[0:1, tb * 512:(tb + 1) * 512], C.onescol,
                               sqap[:, tb * 512:(tb + 1) * 512], start=True, stop=True)
            ins.then_inc(s["nm"], 1)
            done_norm.add(j)

        first_A = True
        for i, (kind, idx) in enumerate(chunks):
            for j in range(i):
                if _NORMED[chunks[j][0]] and j not in done_norm:
                    norms(j)
            e.wait_ge(s["aw"], AW_ALL if kind == "qn" else AW_SMALL)
            pu = prev_user(i)
            if pu is not None:
                e.wait_ge(s["ev"], b["ev"] + ev_at[pu])
                if _NORMED[chunks[pu][0]]:
                    e.wait_ge(s["nr"], b["nr"] + nr_at[pu])
            if first_A and sch["psum_name"][i] == "A":
                e.wait_ge(s["tpf"], 16 * (batch + 1))
                first_A = False
            ins = main_mms(kind, idx, psum_of(i))
            ins.then_inc(s["mm"], 1)
        for j in range(nC):
            if _NORMED[chunks[j][0]] and j not in done_norm:
                norms(j)
        _p2_negb(eng, e, C, b, batch, tot)

    if eng == "vector":
        npair2 = 0
        for i, (kind, idx) in enumerate(chunks):
            X = psum_of(i)
            e.wait_ge(s["mm"], b["mm"] + mm_at[i])
            if kind == "v":
                if idx >= 8:
                    e.wait_ge(s["wrt"], b["wrt"] + 16 * wrt_at[i - 1])
                elif i >= 2:
                    e.wait_ge(s["wrt"], b["wrt"] + 16 * wrt_at[i - 2])
                    e.wait_ge(s["sq"], b["sq"] + sq_at[i - 2])
                e.tensor_copy(vstage(idx), X).then_inc(s["ev"], 1)
            elif kind == "c":
                e.tensor_copy(C.ctb[:, idx * 2048:(idx + 1) * 2048], X).then_inc(s["ev"], 1)
            elif kind in ("qn", "kn"):
                if i >= 2:
                    e.wait_ge(s["wrt"], b["wrt"] + 16 * wrt_at[i - 2])
                    e.wait_ge(s["sq"], b["sq"] + sq_at[i - 2])
                e.tensor_copy(stage_of(i), X).then_inc(s["ev"], 1)
            elif kind in _PAIR1:
                if npair2 >= 1:
                    e.wait_ge(s["wrt"], b["wrt"] + 16 * wrt_at[i - 1])
                    e.wait_ge(s["sq"], b["sq"] + sq_at[i - 1])
                e.tensor_copy(stage_of(i), X).then_inc(s["ev"], 1)
            else:
                e.tensor_copy(stage_of(i), X).then_inc(s["ev"], 1)
                st0 = C.stage[:, 0:2048]
                st1 = C.stage[:, 2048:4096]
                e.wait_ge(s["sq"], b["sq"] + sq_at[i])
                e.tensor_mul(C.tmp0, st0, C.cosb)
                e.tensor_mul(C.tmp1, st1, C.sinb)
                e.tensor_sub(C.tmp0, C.tmp0, C.tmp1)
                e.tensor_mul(C.tmp1, st0, C.sinb)
                e.tensor_mul(st1, st1, C.cosb)
                e.tensor_add(st1, st1, C.tmp1).then_inc(s["rp"], 1)
                npair2 += 1
            if _NORMED[kind]:
                e.wait_ge(s["nm"], b["nm"] + nm_at[i])
                mxcol = {"qn": idx, "qr1": 16 + idx, "qr2": 20 + idx,
                         "kn": 24 + idx, "kr1": 28, "kr2": 29}[kind]
                XB = C.pA if sch["psum_name"][i] == "A" else C.pB
                e.reduce_max(C.mx[0:1, mxcol:mxcol + 1], XB[0:1, 0:2048], AXX
                             ).then_inc(s["nr"], 1)
        _p2_negb(eng, e, C, b, batch, tot)

    if eng == "scalar":
        jn = 0
        for i, (kind, idx) in enumerate(chunks):
            if not _NORMED[kind]:
                continue
            jn += 1
            if jn >= 3:
                e.wait_ge(s["nm"], b["nm"] + jn - 2)
            e.wait_ge(s["ev"], b["ev"] + ev_at[i])
            e.activation(sq_slot(i), stage_of(i), AF.Square).then_inc(s["sq"], 1)
        _p2_negb(eng, e, C, b, batch, tot)

    if eng == "sync":
        npair2 = 0
        for i, (kind, idx) in enumerate(chunks):
            if kind == "qn":
                e.wait_ge(s["ev"], b["ev"] + ev_at[i])
                rows = (batch * H + idx) * 128
                e.dma_start(t["qn_scr"][rows: rows + 128, :], stage_of(i)).then_inc(s["wrt"], 16)
            elif kind == "kn":
                e.wait_ge(s["ev"], b["ev"] + ev_at[i])
                rows = (batch * KV + idx) * 128
                e.dma_start(t["kn_scr"][rows: rows + 128, :], stage_of(i)).then_inc(s["wrt"], 16)
            elif kind == "v" and idx in (7, 15):
                e.wait_ge(s["ev"], b["ev"] + ev_at[i])
                rows = (batch * 2 + idx // 8) * 128
                e.dma_start(t["v_scr"][rows: rows + 128, :], C.stage[:, 0:4096]
                            ).then_inc(s["wrt"], 16)
            elif kind in _PAIR2:
                npair2 += 1
                e.wait_ge(s["rp"], b["rp"] + npair2)
                if kind == "qr2":
                    for hh in range(4):
                        h = idx * 4 + hh
                        rows = (batch * H + h) * RD
                        e.dma_start(t["qr_scr"][rows: rows + 32, :],
                                    C.tmp0[hh * 32:(hh + 1) * 32, :]).then_inc(s["wrt"], 16)
                        e.dma_start(t["qr_scr"][rows + 32: rows + 64, :],
                                    C.stage[hh * 32:(hh + 1) * 32, 2048:4096]).then_inc(s["wrt"], 16)
                else:
                    for g in range(KV):
                        rows = (batch * KV + g) * RD
                        e.dma_start(t["kr_scr"][rows: rows + 32, :],
                                    C.tmp0[g * 32:(g + 1) * 32, :]).then_inc(s["wrt"], 16)
                        e.dma_start(t["kr_scr"][rows + 32: rows + 64, :],
                                    C.stage[g * 32:(g + 1) * 32, 2048:4096]).then_inc(s["wrt"], 16)
        _p2_negb(eng, e, C, b, batch, tot)

    b["aw"] += 16 * (16 + 16 + 16 + 4 + 4 + 4)
    b["mm"] += tot["mm"]
    b["ev"] += tot["ev"]
    b["sq"] += tot["sq"]
    b["nm"] += tot["nm"] + 1
    b["nr"] += tot["nr"] + 1
    b["rp"] += 5 + 15
    b["wrt"] += 16 * tot["wrt"]
    if batch == 0:
        b["cvw"] += CONV_CVW
    return b


def _p2_negb(eng, e, C, b, batch, tot):
    s = C.s
    rp0 = b["rp"] + 5
    if eng == "vector":
        e.wait_ge(s["nr"], b["nr"] + tot["nr"])
        k = rp0

        def step(f):
            nonlocal k
            e.wait_ge(s["rp"], k)
            f().then_inc(s["rp"], 1)
            k += 1

        for c in range(4):
            step(lambda c=c: e.tensor_scalar(C.mx[0:1, 32 + 4 * c: 36 + 4 * c],
                                             C.mx[0:1, 4 * c: 4 * c + 4],
                                             C.mx[0:1, 16 + c: 17 + c], None, ALU.add))
        for c in range(4):
            step(lambda c=c: e.tensor_scalar(C.mx[0:1, 32 + 4 * c: 36 + 4 * c],
                                             C.mx[0:1, 32 + 4 * c: 36 + 4 * c],
                                             C.mx[0:1, 20 + c: 21 + c], None, ALU.add))
        step(lambda: e.tensor_scalar(C.mx[0:1, 48:52], C.mx[0:1, 24:28],
                                     C.mx[0:1, 28:29], None, ALU.add))
        step(lambda: e.tensor_scalar(C.mx[0:1, 48:52], C.mx[0:1, 48:52],
                                     C.mx[0:1, 29:30], None, ALU.add))
        for c in range(4):
            step(lambda c=c: e.tensor_scalar(C.mx[0:1, 32 + 4 * c: 36 + 4 * c],
                                             C.mx[0:1, 32 + 4 * c: 36 + 4 * c],
                                             C.mx[0:1, 48 + c: 49 + c], None, ALU.mult))
        e.wait_ge(s["rp"], k + 1)
        e.wait_ge(s["nm"], b["nm"] + tot["nm"] + 1)
        e.tensor_copy(C.negB[:, batch * H:(batch + 1) * H], C.pA[:, 0:16]
                      ).then_inc(s["nr"], 1)
    if eng == "scalar":
        e.wait_ge(s["rp"], rp0 + 14)
        e.activation(C.mx[0:1, 48:64], C.mx[0:1, 32:48], AF.Sqrt)
        e.mul(C.mx[0:1, 48:64], C.mx[0:1, 48:64], -SCALE).then_inc(s["rp"], 1)
    if eng == "tensor":
        e.wait_ge(s["rp"], rp0 + 15)
        e.matmul(C.pA[:, 0:16], C.onesrow, C.mx[0:1, 48:64], start=True, stop=True
                 ).then_inc(s["nm"], 1)


def _conv_pieces():
    pieces = []
    for srcn, dstn, rows in (("wg32", "wg_scr", 2048), ("wu32", "wu_scr", 2048),
                             ("wd32", "wd_scr", 16384)):
        step = 32 if rows == 2048 else 256
        for r0 in range(0, rows, step):
            pieces.append((srcn, dstn, r0, r0 + step))
    return pieces


_CONVP = _conv_pieces()
CONV_CVW = len(_CONVP) * 16


def _emit_conv_range(e, C, lo, hi):
    s = C.s
    t = C.t
    for j in range(lo, min(hi, len(_CONVP))):
        srcn, dstn, r0, r1 = _CONVP[j]
        e.dma_start(t[dstn][r0:r1, :], t[srcn][r0:r1, :]).then_inc(s["cvw"], 16)



def _emit_ffn(eng, e, C, b, batch):
    s = C.s
    t = C.t
    _gate(e, C, b, GATE_FFN)
    for blk in range(NBLK):
        _emit_ffn_block(eng, e, C, b, batch, blk)
        for k, d in (("w", 32 * NQUAD), ("wf", NQUAD), ("g", NQUAD), ("sil", NQUAD),
                     ("ubc", NQUAD), ("hm", NQUAD), ("hq", NQUAD),
                     ("wd", 16 * 2 * NGRP), ("wdf", 2 * NGRP), ("ye", 2),
                     ("xs", 16), ("yo", 16 * 4)):
            b[k] += d
    return b


def _emit_ffn_block(eng, e, C, b, batch, blk):
    s = C.s
    t = C.t
    pg, pu = C.pA, C.pB
    first = (batch == 0 and blk == 0)

    if eng == "gpsimd":
        if not first:
            e.wait_ge(s["wf"], b["wf"])
        rows = (batch * NBLK + blk) * 128
        e.dma_start(C.xnslot, t["xnT_scr"][rows: rows + 128, :]).then_inc(s["xs"], 16)

    if eng == "sync":
        for q in range(NQUAD):
            slot = q % 2
            prev_user = b["wf"] + q - 2
            if prev_user >= 0:
                e.wait_ge(s["wf"], prev_user + 1)
            cbase = slot * 2 * NH * 512
            e.dma_start(C.wgwu[:, cbase: cbase + NH * 512],
                        t["wg_scr"][q * 128:(q + 1) * 128, :]).then_inc(s["w"], 16)
            e.dma_start(C.wgwu[:, cbase + NH * 512: cbase + 2 * NH * 512],
                        t["wu_scr"][q * 128:(q + 1) * 128, :]).then_inc(s["w"], 16)
        for pi in range(2):
            for grp in range(NGRP):
                i = pi * NGRP + grp
                slot = i % 2
                prev_user = b["wdf"] + i - 2
                if prev_user >= 0:
                    e.wait_ge(s["wdf"], prev_user + 1)
                r0 = (pi * 64 + grp * WD_GRP) * 128
                e.dma_start(C.wdslot[:, slot * WD_GRP * 1024:(slot + 1) * WD_GRP * 1024]
                            .rearrange("p (c n) -> p c n", c=WD_GRP),
                            t["wd_scr"][r0: r0 + WD_GRP * 128, :].rearrange("(c p) n -> p c n", p=128)
                            ).then_inc(s["wd"], 16)

    if eng == "tensor":
        e.wait_ge(s["xs"], b["xs"] + 16)
        for q in range(NQUAD):
            slot = q % 2
            cbase = slot * 2 * NH * 512
            e.wait_ge(s["w"], b["w"] + 32 * (q + 1))
            if b["sil"] + q >= 1:
                e.wait_ge(s["sil"], b["sil"] + q)
            if q == 0 and b["ye"] >= 2:
                e.wait_ge(s["ye"], b["ye"])
            for mc in range(4):
                for hc in range(NH):
                    ins = e.matmul(pg[:, mc * 512:(mc + 1) * 512],
                                   C.wgwu[:, cbase + hc * 512 + mc * 128: cbase + hc * 512 + mc * 128 + 128],
                                   C.xnslot[:, hc * 512:(hc + 1) * 512],
                                   start=(hc == 0), stop=(hc == NH - 1))
            ins.then_inc(s["g"], 1)
            if b["ubc"] + q >= 1:
                e.wait_ge(s["ubc"], b["ubc"] + q)
            for mc in range(4):
                for hc in range(NH):
                    ins = e.matmul(pu[:, mc * 512:(mc + 1) * 512],
                                   C.wgwu[:, cbase + NH * 512 + hc * 512 + mc * 128: cbase + NH * 512 + hc * 512 + mc * 128 + 128],
                                   C.xnslot[:, hc * 512:(hc + 1) * 512],
                                   start=(hc == 0), stop=(hc == NH - 1))
            ins.then_inc(s["wf"], 1)
        e.wait_ge(s["hq"], b["hq"] + NQUAD)
        e.wait_ge(s["sil"], b["sil"] + NQUAD)
        e.wait_ge(s["ubc"], b["ubc"] + NQUAD)
        for pi in range(2):
            if pi == 1:
                e.wait_ge(s["ye"], b["ye"] + 1)
            for grp in range(NGRP):
                i = pi * NGRP + grp
                slot = i % 2
                e.wait_ge(s["wd"], b["wd"] + 16 * (i + 1))
                for ci in range(WD_GRP):
                    Ic = grp * WD_GRP + ci
                    for tokb in range(4):
                        for hgi in range(2):
                            pt = pg if tokb < 2 else pu
                            bank = (tokb % 2) * 2 + hgi
                            ins = e.matmul(
                                pt[:, bank * 512:(bank + 1) * 512],
                                C.h_sp[:, Ic * 512 + tokb * 128: Ic * 512 + tokb * 128 + 128],
                                C.wdslot[:, slot * WD_GRP * 1024 + ci * 1024 + hgi * 512:
                                         slot * WD_GRP * 1024 + ci * 1024 + hgi * 512 + 512],
                                start=(Ic == 0), stop=(Ic == 63))
                ins.then_inc(s["wdf"], 1)

    if eng == "scalar":
        for q in range(NQUAD):
            e.wait_ge(s["g"], b["g"] + q + 1)
            if b["hm"] + q >= 1:
                e.wait_ge(s["hm"], b["hm"] + q)
            for mc in range(4):
                ins = e.activation(C.sg[:, mc * 512:(mc + 1) * 512],
                                   pg[:, mc * 512:(mc + 1) * 512], AF.Silu)
            ins.then_inc(s["sil"], 1)
        for pi in range(2):
            e.wait_ge(s["wdf"], b["wdf"] + (pi + 1) * NGRP)
            if pi == 0 and b["yo"] > 0:
                e.wait_ge(s["yo"], b["yo"])
            for tokb in range(4):
                for hgi in range(2):
                    pt = pg if tokb < 2 else pu
                    bank = (tokb % 2) * 2 + hgi
                    hg = pi * 2 + hgi
                    ins = e.copy(C.ystage[:, tokb * 2048 + hg * 512: tokb * 2048 + hg * 512 + 512],
                                 pt[:, bank * 512:(bank + 1) * 512])
            ins.then_inc(s["ye"], 1)
        for tokb in range(4):
            rows = (batch * 16 + blk * 4 + tokb) * 128
            e.dma_start(t["yf_scr"][rows: rows + 128, :],
                        C.ystage[:, tokb * 2048:(tokb + 1) * 2048]).then_inc(s["yo"], 16)

    if eng == "vector":
        for q in range(NQUAD):
            if q == 0 and b["wdf"] >= 1:
                e.wait_ge(s["wdf"], b["wdf"])
            e.wait_ge(s["wf"], b["wf"] + q + 1)
            for mc in range(4):
                ins = e.tensor_copy(C.ub[:, mc * 512:(mc + 1) * 512],
                                    pu[:, mc * 512:(mc + 1) * 512])
            ins.then_inc(s["ubc"], 1)
            e.wait_ge(s["sil"], b["sil"] + q + 1)
            hs = [C.h_sp[:, (q * 4 + m) * 512:(q * 4 + m + 1) * 512] for m in range(4)]
            for m in range(4):
                ins = e.tensor_mul(hs[m], C.sg[:, m * 512:(m + 1) * 512],
                                   C.ub[:, m * 512:(m + 1) * 512])
            ins.then_inc(s["hm"], 1)
            sqs = [C.sq4[:, m * 512:(m + 1) * 512] for m in range(4)]
            for m in range(4):
                e.tensor_mul(sqs[m], hs[m], hs[m])
            cmp0 = C.cmp2[:, 0:512]
            cmp1 = C.cmp2[:, 512:1024]
            wins = [C.ub[:, m * 512:(m + 1) * 512] for m in range(4)]
            for m in range(4):
                ops = []
                for j in range(4):
                    if j == m:
                        continue
                    op = ALU.is_ge if m < j else ALU.is_gt
                    ops.append((j, op))
                e.tensor_tensor(cmp0, sqs[m], sqs[ops[0][0]], ops[0][1])
                e.tensor_tensor(cmp1, sqs[m], sqs[ops[1][0]], ops[1][1])
                e.tensor_add(wins[m], cmp0, cmp1)
                e.tensor_tensor(cmp0, sqs[m], sqs[ops[2][0]], ops[2][1])
                e.tensor_add(wins[m], wins[m], cmp0)
            for m in range(4):
                ins = e.scalar_tensor_tensor(hs[m], wins[m], 1.5, hs[m], ALU.is_ge, ALU.mult)
            ins.then_inc(s["hq"], 1)



def _atta_sched():
    tiles = []
    head_end = {}
    group_end = {}
    hqb_last = {}
    g = 0
    for h in range(H):
        for qb in range(4):
            for kt in range(4 * qb + 4):
                masked = kt >= 4 * qb
                tiles.append((h, qb, kt, masked, g))
                hqb_last[(h, qb)] = g
                g += 1
        head_end[h] = g
        if h % 4 == 3:
            group_end[h // 4] = g
    em_at = {}
    av_at = {}
    cem = 0
    for (h, qb, kt, masked, gi) in tiles:
        if masked:
            cem += 1
        em_at[gi] = cem
        av_at[gi] = gi + 1
    return tiles, head_end, group_end, hqb_last, em_at


_ATILES, _AHEND, _AGEND, _AHQBL, _AEM = _atta_sched()
NTILE = len(_ATILES)
NMASK = _AEM[_ATILES[-1][4]]


def _emit_atta(eng, e, C, b, batch):
    s = C.s
    t = C.t
    _gate(e, C, b, GATE_ATTA)
    tiles = _ATILES

    if eng == "sync":
        for j in range(2):
            rows = (batch * 2 + j) * 128
            e.dma_start(C.vsb[:, j * 4096:(j + 1) * 4096],
                        t["v_scr"][rows: rows + 128, :]).then_inc(s["kv"], 16)
        for h in range(H):
            g = h // 4
            if h % 4 == 0:
                if g >= 2:
                    e.wait_ge(s["sc"], b["sc"] + _AGEND[g - 2])
                rows = (batch * KV + g) * 128
                e.dma_start(C.karena[:, (g % 2) * 2048:((g % 2) + 1) * 2048],
                            t["kn_scr"][rows: rows + 128, :]).then_inc(s["kv"], 16)
                rows = (batch * KV + g) * RD
                e.dma_start(C.krarena[0:64, (g % 2) * 2048:((g % 2) + 1) * 2048],
                            t["kr_scr"][rows: rows + RD, :]).then_inc(s["kv"], 16)
            if h >= 2:
                e.wait_ge(s["sc"], b["sc"] + _AHEND[h - 2])
            rows = (batch * H + h) * 128
            e.dma_start(C.qn_a[:, (h % 2) * 2048:((h % 2) + 1) * 2048],
                        t["qn_scr"][rows: rows + 128, :]).then_inc(s["qh"], 16)
            rows = (batch * H + h) * RD
            e.dma_start(C.qr_a[0:64, (h % 2) * 2048:((h % 2) + 1) * 2048],
                        t["qr_scr"][rows: rows + RD, :]).then_inc(s["qh"], 16)

    if eng == "tensor":
        pend = []
        prev = None
        nhqb = 0

        def avz(pt):
            h, qb, kt, masked, gi, n = pt
            g = h // 4
            slot = gi % 2
            e.wait_ge(s["ex"], b["ex"] + gi + 1)
            if masked:
                e.wait_ge(s["em"], b["em"] + _AEM[gi])
            if kt == 0 and n >= 1:
                e.wait_ge(s["on"], b["on"] + n - 1)
                e.wait_ge(s["zr"], b["zr"] + 2 * (n - 1) + 1)
            E = C.esb[:, slot * 512:(slot + 1) * 512]
            pO = C.pO[:, (n % 2) * 512:((n % 2) + 1) * 512]
            last = (gi == _AHQBL[(h, qb)])
            e.matmul(pO, C.vsb[:, kt * 512 + (g % 4) * 128: kt * 512 + (g % 4) * 128 + 128],
                     E, start=(kt == 0), stop=last)
            e.matmul(C.pZ[0:1, 0:512], C.onescol, E,
                     start=(kt == 0), stop=last).then_inc(s["av"], 1)

        for (h, qb, kt, masked, gi) in tiles:
            g = h // 4
            slot = gi % 2
            pS = C.pS[:, slot * 512:(slot + 1) * 512]
            if kt == 0:
                e.wait_ge(s["kv"], b["kv"] + 16 * (2 + 2 * (g + 1)))
                e.wait_ge(s["qh"], b["qh"] + 16 * 2 * (h + 1))
            if gi >= 2:
                e.wait_ge(s["ex"], b["ex"] + gi - 1)
            e.matmul(pS, C.karena[:, (g % 2) * 2048 + kt * 128:(g % 2) * 2048 + kt * 128 + 128],
                     C.qn_a[:, (h % 2) * 2048 + qb * 512:(h % 2) * 2048 + (qb + 1) * 512],
                     start=True, stop=False)
            e.matmul(pS, C.krarena[:, (g % 2) * 2048 + kt * 128:(g % 2) * 2048 + kt * 128 + 128],
                     C.qr_a[:, (h % 2) * 2048 + qb * 512:(h % 2) * 2048 + (qb + 1) * 512],
                     start=False, stop=True).then_inc(s["sc"], 1)
            if pend:
                n = pend.pop(0)
                e.wait_ge(s["zr"], b["zr"] + 2 * n + 1)
                e.wait_ge(s["on"], b["on"] + n)
                e.matmul(C.pBC, C.onesrowb, C.rowbufb[0:1, 0:512],
                         start=True, stop=True).then_inc(s["zr"], 1)
            if prev is not None:
                avz(prev)
                if prev[4] == _AHQBL[(prev[0], prev[1])]:
                    pend.append(prev[5])
            prev = (h, qb, kt, masked, gi, nhqb)
            if gi == _AHQBL[(h, qb)]:
                nhqb += 1
        avz(prev)
        pend.append(prev[5])
        for n in pend:
            e.wait_ge(s["zr"], b["zr"] + 2 * n + 1)
            e.wait_ge(s["on"], b["on"] + n)
            e.matmul(C.pBC, C.onesrowb, C.rowbufb[0:1, 0:512],
                     start=True, stop=True).then_inc(s["zr"], 1)

    if eng == "scalar":
        for (h, qb, kt, masked, gi) in tiles:
            e.wait_ge(s["sc"], b["sc"] + gi + 1)
            if gi >= 2:
                e.wait_ge(s["av"], b["av"] + gi - 1)
            slot = gi % 2
            e.activation(C.esb[:, slot * 512:(slot + 1) * 512],
                         C.pS[:, slot * 512:(slot + 1) * 512], AF.Exp,
                         bias=C.negB[:, batch * H + h: batch * H + h + 1],
                         scale=SCALE).then_inc(s["ex"], 1)

    if eng == "vector":
        e.memset(C.krarena[64:128, 0:2048], 0.0)
        e.memset(C.krarena[64:128, 2048:4096], 0.0).then_inc(s["lns"], 2)
        nhqb = 0
        for (h, qb, kt, masked, gi) in tiles:
            slot = gi % 2
            if masked:
                e.wait_ge(s["ex"], b["ex"] + gi + 1)
                dtile = kt - 4 * qb
                e.tensor_mul(C.esb[:, slot * 512:(slot + 1) * 512],
                             C.esb[:, slot * 512:(slot + 1) * 512],
                             C.bmask[:, dtile * 512:(dtile + 1) * 512]).then_inc(s["em"], 1)
            if gi == _AHQBL[(h, qb)]:
                e.wait_ge(s["av"], b["av"] + gi + 1)
                e.reciprocal(C.rowbuf[0:1, 0:512], C.pZ[0:1, 0:512])
                e.tensor_copy(C.rowbufb[0:1, 0:512], C.rowbuf[0:1, 0:512]).then_inc(s["zr"], 1)
                e.wait_ge(s["zr"], b["zr"] + 2 * nhqb + 2)
                e.tensor_copy(C.rbsb, C.pBC)
                e.tensor_tensor(C.obuf[:, h * 2048 + qb * 512: h * 2048 + (qb + 1) * 512],
                                C.pO[:, (nhqb % 2) * 512:((nhqb % 2) + 1) * 512],
                                C.rbsb, ALU.mult).then_inc(s["on"], 1)
                nhqb += 1

    for k, d in (("kv", 16 * 10), ("qh", 16 * 32), ("sc", NTILE), ("ex", NTILE),
                 ("em", NMASK), ("av", NTILE), ("zr", 2 * 64), ("on", 64), ("lns", 2)):
        b[k] += d
    return b



def _emit_attb(eng, e, C, b, batch):
    s = C.s
    t = C.t
    _gate(e, C, b, GATE_ATTB)

    if eng == "gpsimd":
        for hh in range(H):
            e.dma_start(C.wosb[:, hh * 2048:(hh + 1) * 2048],
                        t["won"][hh * 128:(hh + 1) * 128, :]).then_inc(s["wol"], 16)

    if eng == "sync":
        for w in range(8):
            for tt in range(2):
                tokb = w * 2 + tt
                if tokb >= 2:
                    e.wait_ge(s["yq"], b["yq"] + tokb - 1)
                rows = (batch * 16 + tokb) * 128
                e.dma_start(C.yfslot[:, (tokb % 2) * 2048:((tokb % 2) + 1) * 2048],
                            t["yf_scr"][rows: rows + 128, :]).then_inc(s["yw"], 16)
                e.dma_start(C.xtile[:, (tokb % 2) * 2048:((tokb % 2) + 1) * 2048],
                            t["x"][batch * S + tokb * 128: batch * S + (tokb + 1) * 128, :]
                            ).then_inc(s["yw"], 16)
            for tt in range(2):
                tokb = w * 2 + tt
                e.wait_ge(s["yq"], b["yq"] + tokb + 1)
                e.dma_start(t["out"][batch * S + tokb * 128: batch * S + (tokb + 1) * 128, :],
                            C.ostage[:, tt * 2048:(tt + 1) * 2048]).then_inc(s["outd"], 16)

    if eng == "tensor":
        for w in range(8):
            if w >= 1:
                e.wait_ge(s["yq"], b["yq"] + 2 * w)
            for hh in range(H):
                if w == 0:
                    e.wait_ge(s["wol"], b["wol"] + 16 * (hh + 1))
                for tt in range(2):
                    pt = C.pA if tt == 0 else C.pB
                    for hg in range(4):
                        ins = e.matmul(
                            pt[:, hg * 512:(hg + 1) * 512],
                            C.obuf[:, hh * 2048 + w * 256 + tt * 128: hh * 2048 + w * 256 + tt * 128 + 128],
                            C.wosb[:, hh * 2048 + hg * 512: hh * 2048 + (hg + 1) * 512],
                            start=(hh == 0), stop=(hh == H - 1))
            ins.then_inc(s["wo"], 1)

    if eng == "vector":
        for w in range(8):
            e.wait_ge(s["wo"], b["wo"] + w + 1)
            for tt in range(2):
                tokb = w * 2 + tt
                pt = C.pA if tt == 0 else C.pB
                e.wait_ge(s["yw"], b["yw"] + 32 * (tokb + 1))
                if w >= 1:
                    e.wait_ge(s["outd"], b["outd"] + 16 * (2 * (w - 1) + tt + 1))
                e.tensor_add(C.ostage[:, tt * 2048:(tt + 1) * 2048], pt[:, 0:2048],
                             C.yfslot[:, (tokb % 2) * 2048:((tokb % 2) + 1) * 2048])
                e.tensor_add(C.ostage[:, tt * 2048:(tt + 1) * 2048],
                             C.ostage[:, tt * 2048:(tt + 1) * 2048],
                             C.xtile[:, (tokb % 2) * 2048:((tokb % 2) + 1) * 2048]
                             ).then_inc(s["yq"], 1)

    for k, d in (("wol", 16 * H), ("yw", 16 * 32), ("wo", 8), ("yq", 16),
                 ("outd", 16 * 16)):
        b[k] += d
    return b



def _build_program():
    from contextlib import ExitStack
    nc = bass.Bass("TRN2", target_bir_lowering=False)
    stack = ExitStack()
    with stack:
        C = _declare(nc, stack)
        _alloc_overlays(nc, C)
        with nc.Block() as block:
            def body(eng):
                def f(e):
                    s = C.s
                    if eng == "gpsimd":
                        for ap, src in ((C.identb, C.t["identb"]), (C.identf, C.t["identf"]),
                                        (C.onescol, C.t["onescol"]), (C.onesrow, C.t["onesrow"]),
                                        (C.onesrowb, C.t["onesrowb"]),
                                        (C.cosb, C.t["cosb"]), (C.sinb, C.t["sinb"]),
                                        (C.bmask, C.t["bmask"])):
                            e.dma_start(ap, src).then_inc(s["cv"], 16)
                        e.memset(C.epsap, EPS).then_inc(s["cv"], 1)
                    b = _zero_counts()
                    b["cv"] = 8 * 16 + 1
                    if eng in ("tensor", "vector", "scalar"):
                        e.wait_ge(s["cv"], 8 * 16 + 1)
                    for batch in range(BPC):
                        if "p1" in PHASES:
                            b = _emit_p1(eng, e, C, b, batch)
                        if "p2" in PHASES:
                            b = _emit_p2(eng, e, C, b, batch)
                        if "ffn" in PHASES:
                            b = _emit_ffn(eng, e, C, b, batch)
                        if "att" in PHASES:
                            b = _emit_atta(eng, e, C, b, batch)
                            b = _emit_attb(eng, e, C, b, batch)
                    if eng == "gpsimd":
                        for n in ("xnw", "wrt", "cvw", "yo", "outd"):
                            if b[n] > 0:
                                e.wait_ge(s[n], b[n])
                return f
            block.gpsimd(body("gpsimd"))
            block.tensor(body("tensor"))
            block.vector(body("vector"))
            block.scalar(body("scalar"))
            block.sync(body("sync"))
    return nc


def _get_nc():
    if "nc" not in _CACHE:
        _CACHE["nc"] = _build_program()
    return _CACHE["nc"]


def kernel(**inputs):
    in_maps = _host_prep(inputs)
    nc = _get_nc()
    res = run_bass_kernel_spmd(nc, in_maps, core_ids=list(range(NCORES)))
    out = np.empty((B_ALL, S, D), np.float32)
    for cid in range(NCORES):
        out[BPC * cid: BPC * (cid + 1)] = res.results[cid]["out"].reshape(BPC, S, D)
    if DEBUG_DUMPS:
        kernel.last_results = res.results
    return out
